# version 03
# baseline (speedup 1.0000x reference)
import numpy as np
import ml_dtypes
from contextlib import ExitStack

import concourse.bass as bass
import concourse.mybir as mybir
from concourse.bass_utils import run_bass_kernel_spmd

F32 = mybir.dt.float32
BF16 = mybir.dt.bfloat16
I32 = mybir.dt.int32
U32 = mybir.dt.uint32
ALU = mybir.AluOpType
ACT = mybir.ActivationFunctionType
AX = mybir.AxisListType.X

D = 2048
T = 8192
NCORE = 8
TC = T // NCORE
HD = 128
NDC = D // 128
NPT = TC // 128
NPTG = T // 128
IF = 8192
NIC = IF // 128
SCALE = float(HD) ** -0.5
C_EXP = 12.0
NQB = T // 512


class Trk:

    def __init__(self):
        self.w, self.r = set(), set()

    @staticmethod
    def _n(aps):
        s = set()
        for ap in aps:
            t = getattr(ap, "tensor", None)
            if t is not None:
                s.add(t.name)
            elif hasattr(ap, "name") and hasattr(ap, "shape"):
                s.add(ap.name)
        return s

    def pre(self, eng, outs, ins):
        no, ni = self._n(outs), self._n(ins)
        if (ni & self.w) or (no & (self.w | self.r)):
            eng.drain()
            self.w.clear()
            self.r.clear()
        self.w |= no
        self.r |= ni

    def clear(self):
        self.w.clear()
        self.r.clear()


def build_program(upto="full", want_taps=False):
    nc = bass.Bass("TRN2", target_bir_lowering=False, debug=False,
                   enable_partition_id=True)
    es = ExitStack()
    TV, TS = Trk(), Trk()
    LVL = ["qkv", "attn", "wo", "full"].index(upto)
    C = {"ld": 0, "lds": 0, "ldg": 0, "v": 0, "s": 0, "cc": 0, "out": 0,
         "fh": 0, "lc": 0, "lw": 0, "lx": 0, "lc1": 0, "lxs": 0, "lxr": 0,
         "lt": 0}

    din = lambda n_, sh, dt: nc.dram_tensor(n_, sh, dt, kind="ExternalInput").ap()
    dint = lambda n_, sh, dt: nc.dram_tensor(n_, sh, dt, kind="Internal").ap()

    x_in = din("x_in", [TC, D], F32)
    wq_in = din("wq_in", [256, D], F32)
    wk_in = din("wk_in", [128, D], F32)
    wv_in = din("wv_in", [128, D], F32)
    woT_in = din("woT_in", [D, D], F32)
    gwT_in = din("gwT_in", [NIC, D, 128], I32)
    uwT_in = din("uwT_in", [NIC, D, 128], I32)
    dwT_in = din("dwT_in", [NIC, 128, D], I32)
    gs_in = din("gs_in", [128, NIC], F32)
    us_in = din("us_in", [128, NIC], F32)
    ds_in = din("ds_in", [128, NDC], F32)
    cos_in = din("cos_in", [T, HD], F32)
    sin_in = din("sin_in", [T, HD], F32)
    ident_in = din("ident_in", [128, 128], BF16)
    tri_in = din("tri_in", [128, 128], BF16)

    y_out = nc.dram_tensor("y_out", [TC, D], F32, kind="ExternalOutput").ap()

    cc1_in = dint("cc1_in", [D, TC], BF16)
    cc1_out = dint("cc1_out", [NCORE * D, TC], BF16)
    ccs_in = dint("ccs_in", [128, NPT], F32)
    ccs_out = dint("ccs_out", [NCORE * 128, NPT], F32)
    cc2_in = dint("cc2_in", [T, 256], BF16)
    cc2_out = dint("cc2_out", [NCORE * T, 256], BF16)
    hscr = dint("hscr", [NIC, 128, 512], BF16)
    wsscr = dint("wsscr", [128, 4], F32)

    taps = {}
    if want_taps:
        tapdef = {"qx": ([128, NPT * D], BF16), "xs": ([128, NPT], F32),
                  "qkv": ([128, 3 * T + 64 * 129], BF16),
                  "attn": ([128, 64 * 256], BF16),
                  "wo": ([128, NPT * D], F32)}
        for name, (sh, dt) in tapdef.items():
            taps[name] = nc.dram_tensor("tap_" + name, sh, dt,
                                        kind="ExternalOutput").ap()

    sbt = lambda st, n_, sh, dt=F32: st.enter_context(nc.sbuf_tensor(n_, sh, dt))
    sbr = lambda st, n_, sh, dt=F32: st.enter_context(
        nc.sbuf_tensor(n_, sh, dt, side="right"))
    pst = lambda st, n_, sh, dt=F32: st.enter_context(nc.psum_tensor(n_, sh, dt))

    ident = sbt(es, "ident", [128, 128], BF16)
    tri = sbt(es, "tri", [128, 128], BF16)
    cp = sbt(es, "cp", [128, 4])
    ones_bf = sbt(es, "ones_bf", [128, 1], BF16)
    onesf_row = sbt(es, "onesf_row", [1, 128], F32)
    m128_bf = sbt(es, "m128_bf", [1, 128], BF16)
    pid_sb = sbt(es, "pid_sb", [1, 1], U32)
    xs_all = sbt(es, "xs_all", [128, NPTG])
    gs_sb = sbt(es, "gs_sb", [128, NIC])
    us_sb = sbt(es, "us_sb", [128, NIC])
    ds_sb = sbt(es, "ds_sb", [128, NDC])

    sem = lambda n_: es.enter_context(nc.semaphore(n_))
    s_ld = sem("s_ld")
    s_lc = sem("s_lc")
    s_lw = sem("s_lw")
    s_lx = sem("s_lx")
    s_lc1 = sem("s_lc1")
    s_lxs = sem("s_lxs")
    s_lxr = sem("s_lxr")
    s_lt = sem("s_lt")
    s_lda = sem("s_lda")
    s_ldb = sem("s_ldb")
    s_ga = sem("s_ga")
    s_gb = sem("s_gb")
    s_da = sem("s_da")
    s_db = sem("s_db")
    s_fla = sem("s_fla")
    s_flb = sem("s_flb")
    s_ha = sem("s_ha")
    s_hb = sem("s_hb")
    s_wsa = sem("s_wsa")
    s_wsb = sem("s_wsb")
    s_tw = sem("s_tw")
    s_lds = sem("s_lds")
    s_ldg = sem("s_ldg")
    s_v = sem("s_v")
    s_s = sem("s_s")
    s_cc = sem("s_cc")
    s_mm = sem("s_mm")
    s_rp = sem("s_rp")
    s_tp = sem("s_tp")
    s_te = sem("s_te")
    s_qb = sem("s_qb")
    s_qk = sem("s_qk")
    s_exp = sem("s_exp")
    s_msk = sem("s_msk")
    s_pv = sem("s_pv")
    s_pvd = sem("s_pvd")
    s_wo = sem("s_wo")
    s_wov = sem("s_wov")
    s_f1 = sem("s_f1")
    s_f2 = sem("s_f2")
    s_f3 = sem("s_f3")
    s_f4 = sem("s_f4")
    s_f5 = sem("s_f5")
    s_f6 = sem("s_f6")
    s_fh = sem("s_fh")
    s_out = sem("s_out")

    def dsync(eng, out, in_, cnt="ld", k=16):
        eng.dma_start(out=out, in_=in_).then_inc(
            {"ld": s_ld, "lds": s_lds, "ldg": s_ldg, "out": s_out,
             "fh": s_fh, "lc": s_lc, "lw": s_lw, "lx": s_lx, "lc1": s_lc1,
             "lxs": s_lxs, "lxr": s_lxr, "lt": s_lt}[cnt], k)
        C[cnt] += k
        return C[cnt]

    with es, nc.Block() as block:
        @block.sync
        def _(sync):
            for dst, src in ((ident, ident_in), (tri, tri_in), (gs_sb, gs_in),
                             (us_sb, us_in), (ds_sb, ds_in)):
                dsync(sync, dst[:], src[:], "lc")
            dsync(sync, pid_sb[:], nc.partition_id_tensor.ap(), "lc")
        LD_CONST = C["lc"]

        @block.vector
        def _(v):
            for j, val in enumerate((1e-5, 1.0, -C_EXP, 0.0)):
                TV.pre(v, [cp[:, j:j + 1]], [])
                v.memset(cp[:, j:j + 1], val)
            TV.pre(v, [ones_bf[:]], [])
            v.memset(ones_bf[:], 1.0)
            TV.pre(v, [onesf_row[:]], [])
            v.memset(onesf_row[:], 1.0)
            TV.pre(v, [m128_bf[:]], [])
            v.memset(m128_bf[:], -128.0)
            v.drain()
            TV.clear()
            v.sem_inc(s_v, 1)
            C["v"] += 1
        V_CONST = C["v"]

        wt_es = ExitStack()
        wT_all = wt_es.enter_context(nc.sbuf_tensor("wT_all", [128, NDC * 512], BF16))
        wsrep_es = ExitStack()
        ws_rep = wsrep_es.enter_context(nc.sbuf_tensor("ws_rep", [128, 512], F32))
        pw_es = ExitStack()
        pw = [pst(pw_es, f"pw{i}", [128, 128], BF16) for i in range(2)]
        NTR = [0]

        w_es = ExitStack()
        w_f = sbt(w_es, "w_f", [128, 4 * D])
        wdq_bf = sbt(w_es, "wdq_bf", [128, 4 * D], BF16)
        wscr = sbt(w_es, "wscr", [128, D])
        wscri = sbt(w_es, "wscri", [128, D], I32)
        colsc = sbt(w_es, "colsc", [128, 16])

        @block.sync
        def _(sync):
            dsync(sync, w_f[:, 0:2 * D].rearrange("p (t d) -> p t d", t=2),
                  wq_in.rearrange("(t p) d -> p t d", p=128), "lw")
            dsync(sync, w_f[:, 2 * D:3 * D], wk_in[:], "lw")
            dsync(sync, w_f[:, 3 * D:4 * D], wv_in[:], "lw")
        LD_W = C["lw"]

        @block.vector
        def _(v):
            v.wait_ge(s_lw, LD_W)
            TV.clear()
            for i in range(4):
                w_t = w_f[:, i * D:(i + 1) * D]
                TV.pre(v, [colsc[:, i:i + 1]], [w_t])
                v.reduce_max(colsc[:, i:i + 1].rearrange("p (a o) -> p a o", a=1),
                             w_t.rearrange("p (a d) -> p a d", a=1), axis=AX,
                             apply_absolute_value=True)
            TV.pre(v, [colsc[:, 4:8]], [colsc[:, 0:4]])
            v.tensor_scalar(colsc[:, 4:8], colsc[:, 0:4], 1.0 / 127.0, 1e-8,
                            ALU.mult, ALU.max)
            TV.pre(v, [colsc[:, 8:12]], [colsc[:, 4:8]])
            v.reciprocal(colsc[:, 8:12], colsc[:, 4:8])
            for i in range(4):
                w_t = w_f[:, i * D:(i + 1) * D]
                TV.pre(v, [wscr[:]], [w_t, colsc[:, 8 + i:9 + i]])
                v.tensor_scalar(wscr[:], w_t, colsc[:, 8 + i:9 + i], None, ALU.mult)
                TV.pre(v, [wscr[:]], [wscr[:]])
                v.tensor_scalar(wscr[:], wscr[:], 12582912.0, 12582912.0,
                                ALU.add, ALU.subtract)
                TV.pre(v, [wscri[:]], [wscr[:]])
                v.tensor_copy(wscri[:], wscr[:])
                TV.pre(v, [wdq_bf[:, i * D:(i + 1) * D]], [wscri[:]])
                v.tensor_copy(wdq_bf[:, i * D:(i + 1) * D], wscri[:])
            v.drain()
            TV.clear()
            v.sem_inc(s_v, 1)
            C["v"] += 1
        V_WQ = C["v"]

        ws_row = w_es.enter_context(nc.sbuf_tensor("ws_row", [1, 512], F32))
        pwf = w_es.enter_context(nc.psum_tensor("pwf", [128, 512], F32))

        @block.sync
        def _(sync):
            sync.wait_ge(s_v, V_WQ)
            sync.dma_start(out=wsscr[:], in_=colsc[:, 4:8]).then_inc(s_wsa, 16)
            sync.wait_ge(s_wsa, 16)
            sync.dma_start(
                out=ws_row[0:1, :].rearrange("a (i p) -> a i p", i=4),
                in_=wsscr.rearrange("(a p) i -> a i p", a=1),
                allow_slow_non_contiguous=True).then_inc(s_wsb, 16)

        @block.tensor
        def _(t):
            t.wait_ge(s_wsb, 16)
            t.wait_ge(s_v, V_CONST)
            t.matmul(pwf[:], onesf_row[0:1, :], ws_row[0:1, :],
                     start=True, stop=True).then_inc(s_tw, 1)

        @block.vector
        def _(v):
            v.wait_ge(s_tw, 1)
            TV.clear()
            TV.pre(v, [ws_rep[:]], [pwf])
            v.tensor_copy(ws_rep[:], pwf[:])
            v.drain()
            TV.clear()
            v.sem_inc(s_v, 1)
            C["v"] += 1
        V_WSREP = C["v"]

        def emit_transposes(items, wait_v, psums):
            base = NTR[0]
            n_items = len(items)

            @block.tensor
            def _(t):
                t.wait_ge(s_v, wait_v)
                t.wait_ge(s_lc, LD_CONST)
                for k in range(n_items):
                    m = base + k
                    if m >= 2:
                        t.wait_ge(s_te, m - 1)
                    t.transpose(psums[m % 2][:], items[k][0],
                                ident[:]).then_inc(s_tp, 1)

            @block.vector
            def _(v):
                for k in range(n_items):
                    m = base + k
                    v.wait_ge(s_tp, m + 1)
                    TV.clear()
                    TV.pre(v, [items[k][1]], [])
                    v.tensor_copy(items[k][1], psums[m % 2][:]).then_inc(s_te, 1)

            NTR[0] += n_items

        wtr = []
        for i in range(4):
            for dc in range(NDC):
                wtr.append((wdq_bf[:, i * D + dc * 128:i * D + dc * 128 + 128],
                            wT_all[:, dc * 512 + i * 128:dc * 512 + i * 128 + 128]))
        emit_transposes(wtr, V_WQ, pw)

        @block.vector
        def _(v):
            v.wait_ge(s_te, NTR[0])
            TV.clear()
            v.sem_inc(s_v, 1)
            C["v"] += 1
        V_WT = C["v"]
        w_es.close()

        a_es = ExitStack()
        xa = sbt(a_es, "xa", [128, NPT * D])
        qx_sb = sbt(a_es, "qx_sb", [128, NPT * D], BF16)
        qxT_own = sbt(a_es, "qxT_own", [128, NDC * TC], BF16)
        scr_a = sbt(a_es, "scr_a", [128, D])
        scr_i = sbt(a_es, "scr_i", [128, D], I32)
        lncol = sbt(a_es, "lncol", [128, 40])
        xs_own = sbt(a_es, "xs_own", [128, NPT])

        @block.sync
        def _(sync):
            sync.wait_ge(s_v, V_WT)
            dsync(sync, xa[:].rearrange("p (t d) -> p t d", t=NPT),
                  x_in.rearrange("(t p) d -> p t d", p=128), "lx")
        LD_X = C["lx"]

        @block.vector
        def _(v):
            v.wait_ge(s_lx, LD_X)
            TV.clear()
            for p in range(NPT):
                x_t = xa[:, p * D:(p + 1) * D]
                TV.pre(v, [scr_a[:], lncol[:, 8 + p:9 + p]], [x_t])
                v.scalar_tensor_tensor(scr_a[:], x_t, 1.0, x_t, ALU.bypass,
                                       ALU.mult, accum_out=lncol[:, 8 + p:9 + p])
                TV.pre(v, [lncol[:, p:p + 1]], [x_t])
                v.reduce_sum(lncol[:, p:p + 1].rearrange("p (a o) -> p a o", a=1),
                             x_t.rearrange("p (a d) -> p a d", a=1), axis=AX)
                TV.pre(v, [lncol[:, 16 + p:17 + p]], [x_t])
                v.reduce_max(lncol[:, 16 + p:17 + p].rearrange("p (a o) -> p a o", a=1),
                             x_t.rearrange("p (a d) -> p a d", a=1), axis=AX)
                TV.pre(v, [lncol[:, 24 + p:25 + p]], [x_t])
                v.tensor_reduce(lncol[:, 24 + p:25 + p].rearrange("p (a o) -> p a o", a=1),
                                x_t.rearrange("p (a d) -> p a d", a=1), AX, ALU.min)
            TV.pre(v, [lncol[:, 0:8]], [lncol[:, 0:8]])
            v.tensor_scalar(lncol[:, 0:8], lncol[:, 0:8], 1.0 / D, None, ALU.mult)
            TV.pre(v, [lncol[:, 32:40]], [lncol[:, 0:8]])
            v.tensor_tensor(lncol[:, 32:40], lncol[:, 0:8], lncol[:, 0:8], ALU.mult)
            TV.pre(v, [lncol[:, 8:16]], [lncol[:, 8:16], lncol[:, 32:40]])
            v.scalar_tensor_tensor(lncol[:, 8:16], lncol[:, 8:16], 1.0 / D,
                                   lncol[:, 32:40], ALU.mult, ALU.subtract)
            TV.pre(v, [lncol[:, 8:16]], [lncol[:, 8:16]])
            v.tensor_scalar(lncol[:, 8:16], lncol[:, 8:16], 1e-5, None, ALU.add)
            TV.pre(v, [lncol[:, 16:24]], [lncol[:, 16:24], lncol[:, 0:8]])
            v.tensor_tensor(lncol[:, 16:24], lncol[:, 16:24], lncol[:, 0:8],
                            ALU.subtract)
            TV.pre(v, [lncol[:, 24:32]], [lncol[:, 0:8], lncol[:, 24:32]])
            v.tensor_tensor(lncol[:, 24:32], lncol[:, 0:8], lncol[:, 24:32],
                            ALU.subtract)
            TV.pre(v, [lncol[:, 16:24]], [lncol[:, 16:24], lncol[:, 24:32]])
            v.tensor_tensor(lncol[:, 16:24], lncol[:, 16:24], lncol[:, 24:32],
                            ALU.max)
            v.drain()
            TV.clear()
            v.sem_inc(s_v, 1)
            C["v"] += 1
        V_LN1 = C["v"]

        @block.scalar
        def _(s):
            s.wait_ge(s_v, V_CONST)
            s.wait_ge(s_v, V_LN1)
            TS.clear()
            TS.pre(s, [lncol[:, 32:40]], [lncol[:, 8:16], cp])
            s.activation(lncol[:, 32:40], lncol[:, 8:16], ACT.Sqrt,
                         bias=cp[:, 3:4], scale=cp[:, 1:2])
            s.drain()
            TS.clear()
            s.sem_inc(s_s, 1)
            C["s"] += 1
        S_LN1 = C["s"]

        @block.vector
        def _(v):
            v.wait_ge(s_s, S_LN1)
            TV.clear()
            TV.pre(v, [lncol[:, 24:32]], [lncol[:, 32:40]])
            v.reciprocal(lncol[:, 24:32], lncol[:, 32:40])
            TV.pre(v, [xs_own[:]], [lncol[:, 16:24], lncol[:, 24:32]])
            v.tensor_tensor(xs_own[:], lncol[:, 16:24], lncol[:, 24:32], ALU.mult)
            TV.pre(v, [xs_own[:]], [xs_own[:]])
            v.tensor_scalar(xs_own[:], xs_own[:], 1.0 / 127.0, 1e-8,
                            ALU.mult, ALU.max)
            TV.pre(v, [lncol[:, 32:40]], [lncol[:, 16:24]])
            v.reciprocal(lncol[:, 32:40], lncol[:, 16:24])
            TV.pre(v, [lncol[:, 32:40]], [lncol[:, 32:40]])
            v.tensor_scalar(lncol[:, 32:40], lncol[:, 32:40], 127.0, None, ALU.mult)
            TV.pre(v, [lncol[:, 16:24]], [lncol[:, 0:8], lncol[:, 32:40]])
            v.tensor_tensor(lncol[:, 16:24], lncol[:, 0:8], lncol[:, 32:40],
                            ALU.mult)
            for p in range(NPT):
                x_t = xa[:, p * D:(p + 1) * D]
                TV.pre(v, [scr_a[:]],
                       [x_t, lncol[:, 32 + p:33 + p], lncol[:, 16 + p:17 + p]])
                v.tensor_scalar(scr_a[:], x_t, lncol[:, 32 + p:33 + p],
                                lncol[:, 16 + p:17 + p], ALU.mult, ALU.subtract)
                TV.pre(v, [scr_a[:]], [scr_a[:]])
                v.tensor_scalar(scr_a[:], scr_a[:], 12582912.0, 12582912.0,
                                ALU.add, ALU.subtract)
                TV.pre(v, [scr_i[:]], [scr_a[:]])
                v.tensor_copy(scr_i[:], scr_a[:])
                TV.pre(v, [qx_sb[:, p * D:(p + 1) * D]], [scr_i[:]])
                v.tensor_copy(qx_sb[:, p * D:(p + 1) * D], scr_i[:])
            v.drain()
            TV.clear()
            v.sem_inc(s_v, 1)
            C["v"] += 1
        V_QX = C["v"]

        qtr = []
        for p in range(NPT):
            for dc in range(NDC):
                qtr.append((qx_sb[:, p * D + dc * 128:p * D + dc * 128 + 128],
                            qxT_own[:, dc * TC + p * 128:dc * TC + p * 128 + 128]))
        emit_transposes(qtr, V_QX, pw)

        @block.vector
        def _(v):
            v.wait_ge(s_te, NTR[0])
            TV.clear()
            v.sem_inc(s_v, 1)
            C["v"] += 1
        V_QXT = C["v"]

        @block.sync
        def _(sync):
            sync.wait_ge(s_v, V_QXT)
            dsync(sync, cc1_in.rearrange("(dc p) t -> p dc t", p=128),
                  qxT_own[:].rearrange("p (dc t) -> p dc t", dc=NDC), "lc1")
            dsync(sync, ccs_in[:], xs_own[:], "lc1")
            if want_taps:
                dsync(sync, taps["qx"][:], qx_sb[:], "lc1")
                dsync(sync, taps["xs"][:], xs_own[:], "lc1")
        LD_CC1 = C["lc1"]

        @block.gpsimd
        def _(g):
            g.wait_ge(s_lc1, LD_CC1)
            g.collective_compute(
                "AllGather", ALU.bypass, replica_groups=[list(range(NCORE))],
                ins=[cc1_in[:]], outs=[cc1_out[:]]).then_inc(s_cc, 1)
            g.collective_compute(
                "AllGather", ALU.bypass, replica_groups=[list(range(NCORE))],
                ins=[ccs_in[:]], outs=[ccs_out[:]]).then_inc(s_cc, 1)
            C["cc"] += 2
        CC1 = C["cc"]

        @block.sync
        def _(sync):
            sync.wait_ge(s_cc, CC1)
            dsync(sync, xs_all[:].rearrange("p (r q) -> p r q", r=NCORE),
                  ccs_out.rearrange("(r p) q -> p r q", p=128), "lxs")
        LD_XS = C["lxs"]
        a_es.close()

        qkv_es = ExitStack()
        qT_sb = sbr(qkv_es, "qT_sb", [128, 2 * T], BF16)
        kT_sb = sbr(qkv_es, "kT_sb", [128, T], BF16)
        v_sb = sbr(qkv_es, "v_sb", [128, 64 * 129], BF16)

        b_es = ExitStack()
        rk_buf = sbt(b_es, "rk_buf", [128, 2 * NDC * TC], BF16)
        cs_buf = sbt(b_es, "cs_buf", [128, 2 * 2048])
        rp_f = sbt(b_es, "rp_f", [128, 512])
        rp_dq = sbt(b_es, "rp_dq", [128, 512])
        rp_t1 = sbt(b_es, "rp_t1", [128, 192])
        rp_t2 = sbt(b_es, "rp_t2", [128, 192])
        qv_i = sbt(b_es, "qv_i", [128, 512], I32)
        qv_a = sbt(b_es, "qv_a", [128, 512])
        qkv_bf = sbt(b_es, "qkv_bf", [128, 2 * 512], BF16)
        qcols = sbt(b_es, "qcols", [128, 16])
        pqkv = [pst(b_es, f"pqkv{i}", [128, 512]) for i in range(2)]

        @block.vector
        def _(v):
            v.wait_ge(s_cc, CC1)
            TV.clear()
            vo = v_sb[:].rearrange("p (c w) -> p c w", w=129)[:, :, 128:129]
            TV.pre(v, [vo], [])
            v.memset(vo, 1.0)
            v.drain()
            TV.clear()
            v.sem_inc(s_v, 1)
            C["v"] += 1
        V_VONES = C["v"]

        LD_B0 = C["ld"]

        @block.sync
        def _(sync):
            for r in range(NCORE):
                half = r % 2
                sb_sem = s_lda if half == 0 else s_ldb
                sync.wait_ge(s_cc, CC1)
                if r >= 2:
                    sync.wait_ge(s_qb, (r - 1) * NPT)
                sync.dma_start(
                    out=rk_buf[:, half * NDC * TC:(half + 1) * NDC * TC]
                    .rearrange("p (dc t) -> p dc t", dc=NDC),
                    in_=cc1_out[r * D:(r + 1) * D, :]
                    .rearrange("(dc p) t -> p dc t", p=128)).then_inc(sb_sem, 16)
                sync.dma_start(
                    out=cs_buf[:, half * 2048:half * 2048 + 1024]
                    .rearrange("p (q d) -> p q d", q=NPT),
                    in_=cos_in[r * TC:(r + 1) * TC, :]
                    .rearrange("(q p) d -> p q d", p=128)).then_inc(sb_sem, 16)
                sync.dma_start(
                    out=cs_buf[:, half * 2048 + 1024:half * 2048 + 2048]
                    .rearrange("p (q d) -> p q d", q=NPT),
                    in_=sin_in[r * TC:(r + 1) * TC, :]
                    .rearrange("(q p) d -> p q d", p=128)).then_inc(sb_sem, 16)
        LD_RK = lambda r: 48 * (r // 2 + 1)

        base_b = NTR[0]

        @block.tensor
        def _(t):
            t.wait_ge(s_lxs, LD_XS)
            t.wait_ge(s_v, V_QXT)
            for g in range(NPTG + 1):
                if g < NPTG:
                    r, pt = divmod(g, NPT)
                    half, bank = r % 2, g % 2
                    if pt == 0:
                        t.wait_ge(s_lda if r % 2 == 0 else s_ldb, LD_RK(r))
                    if g >= 2:
                        t.wait_ge(s_rp, g - 1)
                    for dc in range(NDC):
                        mm = t.matmul(
                            pqkv[bank][:],
                            rk_buf[:, half * NDC * TC + dc * TC + pt * 128:
                                   half * NDC * TC + dc * TC + pt * 128 + 128],
                            wT_all[:, dc * 512:(dc + 1) * 512],
                            start=(dc == 0), stop=(dc == NDC - 1))
                    mm.then_inc(s_mm, 1)
                if g >= 1:
                    tq = g - 1
                    t.wait_ge(s_qb, tq + 1)
                    for hh in range(3):
                        m = base_b + tq * 3 + hh
                        if m >= 2:
                            t.wait_ge(s_te, m - 1)
                        t.transpose(pw[m % 2][:],
                                    qkv_bf[:, (tq % 2) * 512 + hh * 128:
                                           (tq % 2) * 512 + hh * 128 + 128],
                                    ident[:]).then_inc(s_tp, 1)

        @block.vector
        def _(v):
            v.wait_ge(s_v, V_VONES)
            for g in range(NPTG + 1):
                if g < NPTG:
                    r, pt = divmod(g, NPT)
                    half, bank = r % 2, g % 2
                    v.wait_ge(s_mm, g + 1)
                    TV.clear()
                    TV.pre(v, [rp_dq[:]], [pqkv[bank], ws_rep])
                    v.tensor_tensor(rp_dq[:], pqkv[bank][:], ws_rep[:], ALU.mult)
                    v.drain()
                    TV.clear()
                    v.sem_inc(s_rp, 1)
                    W = rp_dq[:]
                    W3 = W.rearrange("p (h d) -> p h d", h=4)
                    cosb = cs_buf[:, half * 2048 + pt * 128:
                                  half * 2048 + pt * 128 + 128]
                    sinb = cs_buf[:, half * 2048 + 1024 + pt * 128:
                                  half * 2048 + 1024 + pt * 128 + 128]
                    bc = lambda a, w: a.rearrange("p (o d) -> p o d", o=1)\
                        .broadcast_to((128, 3, w))
                    t13 = rp_t1[:].rearrange("p (h d) -> p h d", h=3)
                    t23 = rp_t2[:].rearrange("p (h d) -> p h d", h=3)
                    rp3 = rp_f[:].rearrange("p (h d) -> p h d", h=4)
                    TV.pre(v, [rp_t1[:]], [])
                    v.tensor_tensor(t13, W3[:, 0:3, 0:64], bc(cosb[:, 0:64], 64),
                                    ALU.mult)
                    TV.pre(v, [rp_t2[:]], [])
                    v.tensor_tensor(t23, W3[:, 0:3, 64:128], bc(sinb[:, 0:64], 64),
                                    ALU.mult)
                    TV.pre(v, [rp_f[:]], [rp_t1[:], rp_t2[:]])
                    v.tensor_tensor(rp3[:, 0:3, 0:64], t13, t23, ALU.subtract)
                    TV.pre(v, [rp_t1[:]], [rp_f[:]])
                    v.tensor_tensor(t13, W3[:, 0:3, 64:128], bc(cosb[:, 64:128], 64),
                                    ALU.mult)
                    TV.pre(v, [rp_t2[:]], [rp_t1[:]])
                    v.tensor_tensor(t23, W3[:, 0:3, 0:64], bc(sinb[:, 64:128], 64),
                                    ALU.mult)
                    TV.pre(v, [rp_f[:]], [rp_t1[:], rp_t2[:]])
                    v.tensor_tensor(rp3[:, 0:3, 64:128], t13, t23, ALU.add)
                    TV.pre(v, [rp_f[:]], [rp_dq[:]])
                    v.tensor_copy(rp_f[:, 384:512], W[:, 384:512])
                    TV.pre(v, [qcols[:, 0:4]], [rp_f[:]])
                    v.reduce_max(qcols[:, 0:4].rearrange("p (h o) -> p h o", o=1),
                                 rp_f[:].rearrange("p (h d) -> p h d", h=4),
                                 axis=AX, apply_absolute_value=True)
                    xsb = xs_all[:, g:g + 1].rearrange("p (a o) -> p a o", a=1)\
                        .broadcast_to((128, 4, 1))
                    TV.pre(v, [qcols[:, 4:8]], [qcols[:, 0:4], xs_all])
                    v.tensor_tensor(qcols[:, 4:8].rearrange("p (h o) -> p h o", o=1),
                                    qcols[:, 0:4].rearrange("p (h o) -> p h o", o=1),
                                    xsb, ALU.mult)
                    TV.pre(v, [qcols[:, 4:8]], [qcols[:, 4:8]])
                    v.tensor_scalar(qcols[:, 4:8], qcols[:, 4:8], 1.0 / 127.0,
                                    1e-8, ALU.mult, ALU.max)
                    TV.pre(v, [qcols[:, 8:10]], [qcols[:, 4:6]])
                    v.tensor_scalar(qcols[:, 8:10], qcols[:, 4:6], SCALE, None,
                                    ALU.mult)
                    TV.pre(v, [qcols[:, 10:12]], [qcols[:, 6:8]])
                    v.tensor_copy(qcols[:, 10:12], qcols[:, 6:8])
                    TV.pre(v, [qcols[:, 12:16]], [qcols[:, 0:4]])
                    v.reciprocal(qcols[:, 12:16], qcols[:, 0:4])
                    TV.pre(v, [qcols[:, 12:16]], [qcols[:, 12:16]])
                    v.tensor_scalar(qcols[:, 12:16], qcols[:, 12:16], 127.0, None,
                                    ALU.mult)
                    for h4 in range(4):
                        TV.pre(v, [qv_a[:, h4 * 128:(h4 + 1) * 128]],
                               [rp_f[:], qcols[:, 12 + h4:13 + h4]])
                        v.tensor_scalar(qv_a[:, h4 * 128:(h4 + 1) * 128],
                                        rp_f[:, h4 * 128:(h4 + 1) * 128],
                                        qcols[:, 12 + h4:13 + h4], None, ALU.mult)
                    TV.pre(v, [qv_a[:]], [qv_a[:]])
                    v.tensor_scalar(qv_a[:], qv_a[:], 12582912.0, 12582912.0,
                                    ALU.add, ALU.subtract)
                    TV.pre(v, [qv_i[:]], [qv_a[:]])
                    v.tensor_copy(qv_i[:], qv_a[:])
                    if g >= 2:
                        v.wait_ge(s_tp, base_b + 3 * (g - 1))
                    for h4 in range(4):
                        dst = qkv_bf[:, bank * 512 + h4 * 128:
                                     bank * 512 + (h4 + 1) * 128]
                        TV.pre(v, [dst], [qv_i[:], qcols[:, 8 + h4:9 + h4]])
                        v.tensor_scalar(dst, qv_i[:, h4 * 128:(h4 + 1) * 128],
                                        qcols[:, 8 + h4:9 + h4], None, ALU.mult)
                    dstv = v_sb[:, g * 129:g * 129 + 128]
                    TV.pre(v, [dstv], [qkv_bf])
                    v.tensor_copy(dstv, qkv_bf[:, bank * 512 + 384:
                                               bank * 512 + 512])
                    v.drain()
                    TV.clear()
                    v.sem_inc(s_qb, 1)
                if g >= 1:
                    tq = g - 1
                    for hh in range(3):
                        m = base_b + tq * 3 + hh
                        v.wait_ge(s_tp, m + 1)
                        TV.clear()
                        if hh < 2:
                            dst = qT_sb[:, hh * T + tq * 128:hh * T + tq * 128 + 128]
                        else:
                            dst = kT_sb[:, tq * 128:tq * 128 + 128]
                        TV.pre(v, [dst], [])
                        v.tensor_copy(dst, pw[m % 2][:]).then_inc(s_te, 1)
            v.drain()
            TV.clear()
            v.sem_inc(s_v, 1)
            C["v"] += 1
        V_QKV = C["v"]
        NTR[0] = base_b + NPTG * 3

        if want_taps:
            @block.sync
            def _(sync):
                sync.wait_ge(s_v, V_QKV)
                tp = taps["qkv"]
                dsync(sync, tp[:, 0:2 * T], qT_sb[:], "lt")
                dsync(sync, tp[:, 2 * T:3 * T], kT_sb[:], "lt")
                dsync(sync, tp[:, 3 * T:3 * T + 64 * 129], v_sb[:], "lt")

        def finish(extra=()):
            tot = [(s_ld, C["ld"]), (s_lds, C["lds"]), (s_ldg, C["ldg"]),
                   (s_out, C["out"]), (s_lt, C["lt"]), (s_lc1, C["lc1"]),
                   (s_fh, C["fh"])]

            @block.gpsimd
            def _(g):
                for semh, val in extra:
                    g.wait_ge(semh, val)
                for semh, val in tot:
                    if val:
                        g.wait_ge(semh, val)

        if LVL == 0:
            finish([(s_v, V_QKV)])
            b_es.close()
            pw_es.close()
            wsrep_es.close()
            wt_es.close()
            qkv_es.close()
            return nc, taps

        b_es.close()
        pw_es.close()
        wsrep_es.close()
        wt_es.close()

        c_es = ExitStack()
        pT = [sbr(c_es, f"pT{i}", [128, 1024], BF16) for i in range(2)]
        stp = [pst(c_es, f"st{i}", [128, 1024]) for i in range(2)]
        pv_ps = [pst(c_es, f"pv{j}", [128, 512]) for j in range(4)]
        rcol = sbr(c_es, "rcol", [128, 4])
        attn_sb = c_es.enter_context(
            nc.sbuf_tensor("attn_sb", [128, 64 * 256], BF16, side="right"))

        halves = []
        for qb in range(NQB):
            for h in range(2):
                for kb in range(qb + 1):
                    for hf in range(2):
                        halves.append((qb, h, kb, hf, kb == qb))
        NHALF = len(halves)
        grp_end = {}
        nmsk = [0] * (NHALF + 1)
        ngrp = [0] * (NHALF + 1)
        for n, (qb, h, kb, hf, diag) in enumerate(halves):
            nmsk[n + 1] = nmsk[n] + (2 if diag else 0)
            ngrp[n + 1] = ngrp[n]
            if diag and hf == 1:
                grp_end[n] = (qb, h)
                ngrp[n + 1] += 1

        @block.tensor
        def _(t):
            t.wait_ge(s_v, V_QKV)

            def emit_qk(n):
                qb, h, kb, hf, diag = halves[n]
                if n >= 2:
                    t.wait_ge(s_exp, n - 1)
                for ci in range(2):
                    i = 2 * hf + ci
                    mm = t.matmul(
                        stp[hf][:, ci * 512:(ci + 1) * 512],
                        kT_sb[:, kb * 512 + i * 128:kb * 512 + i * 128 + 128],
                        qT_sb[:, h * T + qb * 512:h * T + qb * 512 + 512],
                        start=True, stop=True)
                mm.then_inc(s_qk, 1)

            def emit_pv(n):
                qb, h, kb, hf, diag = halves[n]
                t.wait_ge(s_exp, n + 1)
                if diag:
                    t.wait_ge(s_msk, nmsk[n + 1])
                if kb == 0 and hf == 0:
                    t.wait_ge(s_pvd, ngrp[n])
                last = None
                for ci in range(2):
                    i = 2 * hf + ci
                    for j in range(4):
                        if diag and j < i:
                            continue
                        last = t.matmul(
                            pv_ps[j][:, 0:129],
                            pT[hf][:, ci * 512 + j * 128:ci * 512 + j * 128 + 128],
                            v_sb[:, (kb * 4 + i) * 129:(kb * 4 + i) * 129 + 129],
                            start=(kb == 0 and i == 0),
                            stop=(diag and i == j))
                last.then_inc(s_pv, 1)

            emit_qk(0)
            for n in range(NHALF):
                if n + 1 < NHALF:
                    emit_qk(n + 1)
                emit_pv(n)

        @block.scalar
        def _(s):
            s.wait_ge(s_v, V_CONST)
            for n in range(NHALF):
                s.wait_ge(s_qk, n + 1)
                if n >= 2:
                    s.wait_ge(s_pv, n - 1)
                TS.clear()
                TS.pre(s, [pT[n % 2][:]], [stp[n % 2][:], cp])
                s.activation(pT[n % 2][:], stp[n % 2][:], ACT.Exp,
                             bias=cp[:, 2:3], scale=cp[:, 1:2]).then_inc(s_exp, 1)

        @block.vector
        def _(v):
            for n, (qb, h, kb, hf, diag) in enumerate(halves):
                if diag:
                    v.wait_ge(s_exp, n + 1)
                    TV.clear()
                    for ci in range(2):
                        i = 2 * hf + ci
                        blk = pT[hf][:, ci * 512 + i * 128:ci * 512 + i * 128 + 128]
                        TV.pre(v, [blk], [blk, tri])
                        v.tensor_tensor(blk, blk, tri[:], ALU.mult).then_inc(s_msk, 1)
                if n in grp_end:
                    v.wait_ge(s_pv, n + 1)
                    TV.clear()
                    for j in range(4):
                        TV.pre(v, [rcol[:, j:j + 1]], [pv_ps[j]])
                        v.reciprocal(rcol[:, j:j + 1], pv_ps[j][:, 128:129])
                        dst = attn_sb[:, (4 * qb + j) * 256 + h * 128:
                                      (4 * qb + j) * 256 + h * 128 + 128]
                        TV.pre(v, [dst], [pv_ps[j], rcol[:, j:j + 1]])
                        v.tensor_scalar(dst, pv_ps[j][:, 0:128],
                                        rcol[:, j:j + 1], None, ALU.mult)
                    v.drain()
                    TV.clear()
                    v.sem_inc(s_pvd, 1)
            v.sem_inc(s_v, 1)
            C["v"] += 1
        V_ATTN = C["v"]

        if want_taps:
            @block.sync
            def _(sync):
                sync.wait_ge(s_v, V_ATTN)
                dsync(sync, taps["attn"][:], attn_sb[:], "lt")
        LT_ATTN = C["lt"]

        if LVL == 1:
            finish([(s_v, V_ATTN)])
            c_es.close()
            qkv_es.close()
            return nc, taps

        @block.gpsimd
        def _(g):
            g.wait_ge(s_v, V_ATTN)
            dsync(g, cc2_in.rearrange("(c p) o -> p c o", p=128),
                  attn_sb[:].rearrange("p (c o) -> p c o", c=64), "ldg")
            g.wait_ge(s_ldg, C["ldg"])
            g.collective_compute(
                "AllGather", ALU.bypass, replica_groups=[list(range(NCORE))],
                ins=[cc2_in[:]], outs=[cc2_out[:]]).then_inc(s_cc, 1)
            C["cc"] += 1
        CC2 = C["cc"]

        c_es.close()
        qkv_es.close()
        x_res = es.enter_context(nc.sbuf_tensor("x_res", [128, NPT * D], F32))
        d_es = ExitStack()
        attnT = sbt(d_es, "attnT", [128, NDC * TC], BF16)
        woT_bf = sbt(d_es, "woT_bf", [128, NDC * D], BF16)
        pwo = [pst(d_es, f"pwo{i}", [128, 512]) for i in range(2)]

        @block.gpsimd
        def _(g):
            if want_taps:
                g.wait_ge(s_lt, LT_ATTN)
            for q in range(4):
                dsync(g, woT_bf[:, q * 4 * D:(q + 1) * 4 * D]
                      .rearrange("p (hc o) -> p hc o", hc=4),
                      woT_in[q * 512:(q + 1) * 512, :]
                      .rearrange("(hc p) o -> p hc o", p=128), "ldg")
        LDG_WO = C["ldg"]

        @block.sync
        def _(sync):
            sync.wait_ge(s_v, V_ATTN)
            dsync(sync, x_res[:].rearrange("p (t d) -> p t d", t=NPT),
                  x_in.rearrange("(t p) d -> p t d", p=128), "lxr")
        LD_XRES = C["lxr"]

        @block.scalar
        def _(s):
            s.wait_ge(s_cc, CC2)
            s.wait_ge(s_lc, LD_CONST)
            TS.clear()
            reg = s.value_load(pid_sb[0:1, 0:1])
            for arm in s.Switch(reg, NCORE):
                for hg in range(16):
                    r, hh = divmod(hg, 2)
                    s.dma_start_transpose(
                        attnT[:, hg * TC:(hg + 1) * TC],
                        cc2_out[r * T + arm * TC:r * T + (arm + 1) * TC,
                                hh * 128:(hh + 1) * 128]).then_inc(s_lds, 16)
            C["lds"] += 16 * 16
        LDS_AT = C["lds"]

        @block.tensor
        def _(t):
            t.wait_ge(s_lds, LDS_AT)
            t.wait_ge(s_ldg, LDG_WO)
            for p in range(NPT):
                for oc in range(4):
                    u = p * 4 + oc
                    if u >= 2:
                        t.wait_ge(s_wov, u - 1)
                    for hg in range(16):
                        mm = t.matmul(
                            pwo[u % 2][:],
                            attnT[:, hg * TC + p * 128:hg * TC + p * 128 + 128],
                            woT_bf[:, hg * D + oc * 512:hg * D + oc * 512 + 512],
                            start=(hg == 0), stop=(hg == 15))
                    mm.then_inc(s_wo, 1)

        @block.vector
        def _(v):
            v.wait_ge(s_lxr, LD_XRES)
            for p in range(NPT):
                for oc in range(4):
                    u = p * 4 + oc
                    v.wait_ge(s_wo, u + 1)
                    TV.clear()
                    xsl = x_res[:, p * D + oc * 512:p * D + oc * 512 + 512]
                    TV.pre(v, [xsl], [xsl, pwo[u % 2]])
                    v.tensor_tensor(xsl, xsl, pwo[u % 2][:], ALU.add
                                    ).then_inc(s_wov, 1)
            v.drain()
            TV.clear()
            v.sem_inc(s_v, 1)
            C["v"] += 1
        V_WO = C["v"]

        if want_taps:
            @block.sync
            def _(sync):
                sync.wait_ge(s_v, V_WO)
                dsync(sync, taps["wo"][:], x_res[:], "lt")
        LT_WO = C["lt"]

        if LVL == 2:
            finish([(s_v, V_WO)])
            d_es.close()
            return nc, taps

        d_es.close()

        e1 = ExitStack()
        xln_bf = sbt(e1, "xln_bf", [128, NPT * D], BF16)
        escr = sbt(e1, "escr", [128, D])
        elncol = sbt(e1, "elncol", [128, 40])
        e2 = ExitStack()
        xlnT = sbr(e2, "xlnT", [128, NDC * TC], BF16)
        epw_es = ExitStack()
        epw = [pst(epw_es, f"epw{i}", [128, 128], BF16) for i in range(2)]

        @block.vector
        def _(v):
            v.wait_ge(s_v, V_WO)
            if want_taps:
                v.wait_ge(s_lt, LT_WO)
            TV.clear()
            for p in range(NPT):
                x_t = x_res[:, p * D:(p + 1) * D]
                TV.pre(v, [escr[:], elncol[:, 8 + p:9 + p]], [x_t])
                v.scalar_tensor_tensor(escr[:], x_t, 1.0, x_t, ALU.bypass,
                                       ALU.mult, accum_out=elncol[:, 8 + p:9 + p])
                TV.pre(v, [elncol[:, p:p + 1]], [x_t])
                v.reduce_sum(elncol[:, p:p + 1].rearrange("p (a o) -> p a o", a=1),
                             x_t.rearrange("p (a d) -> p a d", a=1), axis=AX)
            TV.pre(v, [elncol[:, 0:8]], [elncol[:, 0:8]])
            v.tensor_scalar(elncol[:, 0:8], elncol[:, 0:8], 1.0 / D, None, ALU.mult)
            TV.pre(v, [elncol[:, 32:40]], [elncol[:, 0:8]])
            v.tensor_tensor(elncol[:, 32:40], elncol[:, 0:8], elncol[:, 0:8],
                            ALU.mult)
            TV.pre(v, [elncol[:, 8:16]], [elncol[:, 8:16], elncol[:, 32:40]])
            v.scalar_tensor_tensor(elncol[:, 8:16], elncol[:, 8:16], 1.0 / D,
                                   elncol[:, 32:40], ALU.mult, ALU.subtract)
            TV.pre(v, [elncol[:, 8:16]], [elncol[:, 8:16]])
            v.tensor_scalar(elncol[:, 8:16], elncol[:, 8:16], 1e-5, None, ALU.add)
            v.drain()
            TV.clear()
            v.sem_inc(s_v, 1)
            C["v"] += 1
        V_LN2S = C["v"]

        @block.scalar
        def _(s):
            s.wait_ge(s_v, V_LN2S)
            TS.clear()
            TS.pre(s, [elncol[:, 16:24]], [elncol[:, 8:16], cp])
            s.activation(elncol[:, 16:24], elncol[:, 8:16], ACT.Sqrt,
                         bias=cp[:, 3:4], scale=cp[:, 1:2])
            s.drain()
            TS.clear()
            s.sem_inc(s_s, 1)
            C["s"] += 1
        S_LN2 = C["s"]

        @block.vector
        def _(v):
            v.wait_ge(s_s, S_LN2)
            TV.clear()
            TV.pre(v, [elncol[:, 24:32]], [elncol[:, 16:24]])
            v.reciprocal(elncol[:, 24:32], elncol[:, 16:24])
            for p in range(NPT):
                x_t = x_res[:, p * D:(p + 1) * D]
                dst = xln_bf[:, p * D:(p + 1) * D]
                TV.pre(v, [dst],
                       [x_t, elncol[:, p:p + 1], elncol[:, 24 + p:25 + p]])
                v.tensor_scalar(dst, x_t, elncol[:, p:p + 1],
                                elncol[:, 24 + p:25 + p], ALU.subtract, ALU.mult)
            v.drain()
            TV.clear()
            v.sem_inc(s_v, 1)
            C["v"] += 1
        V_XLN = C["v"]

        etr = []
        for p in range(NPT):
            for dc in range(NDC):
                etr.append((xln_bf[:, p * D + dc * 128:p * D + dc * 128 + 128],
                            xlnT[:, dc * TC + p * 128:dc * TC + p * 128 + 128]))
        emit_transposes(etr, V_XLN, epw)

        @block.vector
        def _(v):
            v.wait_ge(s_te, NTR[0])
            TV.clear()
            v.sem_inc(s_v, 1)
            C["v"] += 1
        V_XLNT = C["v"]
        e1.close()
        e3 = ExitStack()
        h_sb = sbt(e3, "h_sb", [128, NIC * 512], BF16)

        sh_bf = sbt(e3, "sh_bf", [1, 1024], BF16)
        gu_es = ExitStack()
        g_ps = [pst(gu_es, f"g_ps{i}", [128, 512]) for i in range(2)]
        u_ps = [pst(gu_es, f"u_ps{i}", [128, 512]) for i in range(2)]
        s1_ps = [pst(gu_es, f"s1_ps{i}", [1, 512]) for i in range(2)]
        s1row = sbt(gu_es, "s1row", [1, 1024])
        s1_bf = sbt(gu_es, "s1_bf", [1, 1024], BF16)
        gwb = sbt(gu_es, "gwb", [128, 2 * D], BF16)
        uwb = sbt(gu_es, "uwb", [128, 2 * D], BF16)
        sg_sb = [sbt(gu_es, f"sg{i}", [128, 512], BF16) for i in range(2)]
        h1st = [sbt(gu_es, f"h1st{i}", [128, 512], BF16) for i in range(2)]

        @block.tensor
        def _(t):
            t.wait_ge(s_v, V_XLNT)
            for b2 in range(2):
                for dc in range(NDC):
                    mm = t.matmul(s1_ps[b2][:], ones_bf[:],
                                  xlnT[:, dc * TC + b2 * 512:dc * TC + b2 * 512 + 512],
                                  start=(dc == 0), stop=(dc == NDC - 1))
                mm.then_inc(s_f4, 1)

        @block.vector
        def _(v):
            v.wait_ge(s_f4, 2)
            TV.clear()
            for b2 in range(2):
                TV.pre(v, [s1row[:]], [s1_ps[b2]])
                v.tensor_copy(s1row[0:1, b2 * 512:(b2 + 1) * 512], s1_ps[b2][0:1, :])
            TV.pre(v, [s1_bf[:]], [s1row[:]])
            v.tensor_copy(s1_bf[0:1, :], s1row[0:1, :])
            v.drain()
            TV.clear()
            v.sem_inc(s_v, 1)
            C["v"] += 1
        V_S1 = C["v"]

        @block.gpsimd
        def _(g):
            for i in range(NIC):
                gsem = s_ga if i % 2 == 0 else s_gb
                g.wait_ge(s_v, V_XLNT)
                if i >= 2:
                    g.wait_ge(s_f1, 2 * i - 2)
                g.dma_start(out=gwb[:, (i % 2) * D:(i % 2 + 1) * D]
                            .rearrange("p (dc w) -> p dc w", dc=NDC),
                            in_=gwT_in[i].rearrange("(dc p) w -> p dc w", p=128)
                            ).then_inc(gsem, 16)
                g.dma_start(out=uwb[:, (i % 2) * D:(i % 2 + 1) * D]
                            .rearrange("p (dc w) -> p dc w", dc=NDC),
                            in_=uwT_in[i].rearrange("(dc p) w -> p dc w", p=128)
                            ).then_inc(gsem, 16)
        LDG_GU = lambda i: 32 * (i // 2 + 1)

        @block.tensor
        def _(t):
            t.wait_ge(s_v, V_S1)
            for i in range(NIC):
                t.wait_ge(s_ga if i % 2 == 0 else s_gb, LDG_GU(i))
                for b2 in range(2):
                    u = 2 * i + b2
                    if u >= 2:
                        t.wait_ge(s_f3, u - 1)
                    for dc in range(NDC):
                        t.matmul(g_ps[b2][:],
                                 gwb[:, (i % 2) * D + dc * 128:
                                     (i % 2) * D + dc * 128 + 128],
                                 xlnT[:, dc * TC + b2 * 512:dc * TC + b2 * 512 + 512],
                                 start=(dc == 0), stop=False)
                    t.matmul(g_ps[b2][:], m128_bf[0:1, :],
                             s1_bf[0:1, b2 * 512:(b2 + 1) * 512],
                             start=False, stop=True)
                    for dc in range(NDC):
                        t.matmul(u_ps[b2][:],
                                 uwb[:, (i % 2) * D + dc * 128:
                                     (i % 2) * D + dc * 128 + 128],
                                 xlnT[:, dc * TC + b2 * 512:dc * TC + b2 * 512 + 512],
                                 start=(dc == 0), stop=False)
                    mm = t.matmul(u_ps[b2][:], m128_bf[0:1, :],
                                  s1_bf[0:1, b2 * 512:(b2 + 1) * 512],
                                  start=False, stop=True)
                    mm.then_inc(s_f1, 1)
                    if u >= 2:
                        uo = u - 2
                        io, b2o = divmod(uo, 2)
                        t.wait_ge(s_f3, uo + 1)
                        src = (h_sb[:, io * 512:(io + 1) * 512] if b2o == 0
                               else h1st[io % 2][:])
                        mm = t.matmul(s1_ps[b2o][:], ones_bf[:], src,
                                      start=(io == 0), stop=(io == NIC - 1))
                        mm.then_inc(s_f4, 1)
            for uo in (2 * NIC - 2, 2 * NIC - 1):
                io, b2o = divmod(uo, 2)
                t.wait_ge(s_f3, uo + 1)
                src = (h_sb[:, io * 512:(io + 1) * 512] if b2o == 0
                       else h1st[io % 2][:])
                mm = t.matmul(s1_ps[b2o][:], ones_bf[:], src,
                              start=(io == 0), stop=(io == NIC - 1))
                mm.then_inc(s_f4, 1)

        @block.scalar
        def _(s):
            for i in range(NIC):
                for b2 in range(2):
                    u = 2 * i + b2
                    s.wait_ge(s_f1, u + 1)
                    if u >= 2:
                        s.wait_ge(s_f3, u - 1)
                    TS.clear()
                    TS.pre(s, [sg_sb[u % 2][:]], [g_ps[b2], gs_sb])
                    s.activation(sg_sb[u % 2][:], g_ps[b2][:], ACT.Silu,
                                 bias=cp[:, 3:4], scale=gs_sb[:, i:i + 1]
                                 ).then_inc(s_f2, 1)

        @block.vector
        def _(v):
            for i in range(NIC):
                for b2 in range(2):
                    u = 2 * i + b2
                    v.wait_ge(s_f2, u + 1)
                    TV.clear()
                    if b2 == 0:
                        dst = h_sb[:, i * 512:(i + 1) * 512]
                    else:
                        if i >= 2:
                            v.wait_ge(s_ha if i % 2 == 0 else s_hb,
                                      16 * (i // 2))
                            v.wait_ge(s_f4, 2 + 2 * i - 2)
                        dst = h1st[i % 2][:]
                    TV.pre(v, [dst], [u_ps[b2], us_sb, sg_sb[u % 2]])
                    v.scalar_tensor_tensor(dst, u_ps[b2][:], us_sb[:, i:i + 1],
                                           sg_sb[u % 2][:], ALU.mult, ALU.mult
                                           ).then_inc(s_f3, 1)

        @block.sync
        def _(sync):
            for i in range(NIC):
                sync.wait_ge(s_f3, 2 * i + 2)
                sync.dma_start(out=hscr[i], in_=h1st[i % 2][:]).then_inc(
                    s_ha if i % 2 == 0 else s_hb, 16)
        FH_ST = 16 * (NIC // 2)

        @block.vector
        def _(v):
            v.wait_ge(s_f4, 2 + 2 * NIC)
            TV.clear()
            for b2 in range(2):
                TV.pre(v, [s1row[:]], [s1_ps[b2]])
                v.tensor_copy(s1row[0:1, b2 * 512:(b2 + 1) * 512], s1_ps[b2][0:1, :])
            TV.pre(v, [sh_bf[:]], [s1row[:]])
            v.tensor_copy(sh_bf[0:1, :], s1row[0:1, :])
            v.drain()
            TV.clear()
            v.sem_inc(s_v, 1)
            C["v"] += 1
        V_SH = C["v"]

        gu_es.close()
        e2.close()

        dn_es = ExitStack()
        dn_ps = [pst(dn_es, f"dn{j}", [128, 512]) for j in range(4)]
        wdb = [sbt(dn_es, f"wdb{i}", [128, 512], BF16) for i in range(2)]
        h1ld = [sbt(dn_es, f"h1ld{i}", [128, 512], BF16) for i in range(2)]
        stg_bf = sbt(dn_es, "stg_bf", [128, 4 * 512], BF16)

        def dn_cnt(u):
            return 65 * (u // 64) + (u % 64) + 1

        LDG_DN0 = LDG_GU(NIC - 1)
        tr_base = NTR[0]

        @block.gpsimd
        def _(g):
            for gi in range(8 * NIC):
                bq, i = divmod(gi, NIC)
                b, qr = divmod(bq, 4)
                dsem = s_da if gi % 2 == 0 else s_db
                g.wait_ge(s_v, V_SH)
                if gi >= 2:
                    g.wait_ge(s_f6, dn_cnt(gi - 2))
                g.dma_start(out=wdb[gi % 2][:],
                            in_=dwT_in[i][:, qr * 512:(qr + 1) * 512]
                            ).then_inc(dsem, 16)
        LDG_DN = lambda gi: 16 * (gi // 2 + 1)

        @block.sync
        def _(sync):
            sync.wait_ge(s_ha, FH_ST)
            sync.wait_ge(s_hb, FH_ST)
            for li in range(4 * NIC):
                qr, i = divmod(li, NIC)
                ug = (4 + qr) * NIC + i
                fsem = s_fla if li % 2 == 0 else s_flb
                sync.wait_ge(s_v, V_SH)
                if li >= 2:
                    sync.wait_ge(s_f6, dn_cnt(ug - 2))
                sync.dma_start(out=h1ld[i % 2][:], in_=hscr[i]).then_inc(fsem, 16)

        @block.tensor
        def _(t):
            t.wait_ge(s_v, V_SH)
            for bq in range(8):
                b, qr = divmod(bq, 4)
                for i in range(NIC):
                    gi = bq * NIC + i
                    t.wait_ge(s_da if gi % 2 == 0 else s_db, LDG_DN(gi))
                    if b == 1:
                        li = qr * NIC + i
                        t.wait_ge(s_fla if li % 2 == 0 else s_flb,
                                  16 * (li // 2 + 1))
                    if i == 0:
                        t.wait_ge(s_f5, bq * 4)
                    h_src = (h_sb[:, i * 512:(i + 1) * 512] if b == 0
                             else h1ld[i % 2][:])
                    for ocl in range(4):
                        mm = t.matmul(dn_ps[ocl][:],
                                      wdb[gi % 2][:, ocl * 128:(ocl + 1) * 128],
                                      h_src, start=(i == 0), stop=False)
                    mm.then_inc(s_f6, 1)
                for ocl in range(4):
                    mm = t.matmul(dn_ps[ocl][:], m128_bf[0:1, :],
                                  sh_bf[0:1, b * 512:(b + 1) * 512],
                                  start=False, stop=True)
                mm.then_inc(s_f6, 1)
                t.wait_ge(s_f5, bq * 4 + 4)
                for k in range(16):
                    m = tr_base + bq * 16 + k
                    if m >= 2:
                        t.wait_ge(s_te, m - 1)
                    ocl, j4 = divmod(k, 4)
                    t.transpose(epw[m % 2][:],
                                stg_bf[:, ocl * 512 + j4 * 128:
                                       ocl * 512 + j4 * 128 + 128],
                                ident[:]).then_inc(s_tp, 1)

        @block.vector
        def _(v):
            for bq in range(8):
                b, qr = divmod(bq, 4)
                v.wait_ge(s_f6, 65 * bq + 65)
                if bq >= 1:
                    v.wait_ge(s_tp, tr_base + 16 * bq)
                TV.clear()
                for ocl in range(4):
                    oc_g = qr * 4 + ocl
                    dst = stg_bf[:, ocl * 512:(ocl + 1) * 512]
                    TV.pre(v, [dst], [dn_ps[ocl], ds_sb])
                    v.tensor_scalar(dst, dn_ps[ocl][:], ds_sb[:, oc_g:oc_g + 1],
                                    None, ALU.mult).then_inc(s_f5, 1)
                for k in range(16):
                    m = tr_base + bq * 16 + k
                    v.wait_ge(s_tp, m + 1)
                    TV.clear()
                    ocl, j4 = divmod(k, 4)
                    oc_g = qr * 4 + ocl
                    p_own = 4 * b + j4
                    xsl = x_res[:, p_own * D + oc_g * 128:
                                p_own * D + oc_g * 128 + 128]
                    TV.pre(v, [xsl], [xsl, epw[m % 2]])
                    v.tensor_tensor(xsl, xsl, epw[m % 2][:], ALU.add
                                    ).then_inc(s_te, 1)
            v.drain()
            TV.clear()
            v.sem_inc(s_v, 1)
            C["v"] += 1
        V_FFN = C["v"]
        NTR[0] = tr_base + 8 * 16

        @block.sync
        def _(sync):
            sync.wait_ge(s_v, V_FFN)
            dsync(sync, y_out.rearrange("(t p) d -> p t d", p=128),
                  x_res[:].rearrange("p (t d) -> p t d", t=NPT), "out")

        finish([(s_v, V_FFN)])
        dn_es.close()
        epw_es.close()
        e3.close()

    return nc, taps


_CACHE = {}


def _prog(upto="full", want_taps=False):
    key = (upto, want_taps)
    if key not in _CACHE:
        _CACHE[key] = build_program(upto, want_taps)
    return _CACHE[key]


def make_inputs(L0_attn_x, L0_attn_wq, L0_attn_wk, L0_attn_wv, L0_attn_wo,
                L0_ffn_gate_w, L0_ffn_gate_scale, L0_ffn_up_w, L0_ffn_up_scale,
                L0_ffn_down_w, L0_ffn_down_scale):
    bf = ml_dtypes.bfloat16
    x = np.ascontiguousarray(np.asarray(L0_attn_x, dtype=np.float32)[0])
    wq = np.asarray(L0_attn_wq, dtype=np.float32)
    wk = np.asarray(L0_attn_wk, dtype=np.float32)
    wv = np.asarray(L0_attn_wv, dtype=np.float32)
    woT = np.ascontiguousarray(np.asarray(L0_attn_wo, dtype=np.float32).T)
    gw = np.asarray(L0_ffn_gate_w, dtype=np.int32)
    uw = np.asarray(L0_ffn_up_w, dtype=np.int32)
    dw = np.asarray(L0_ffn_down_w, dtype=np.int32)
    gwT = np.ascontiguousarray(gw.T.reshape(D, NIC, 128).transpose(1, 0, 2))
    uwT = np.ascontiguousarray(uw.T.reshape(D, NIC, 128).transpose(1, 0, 2))
    dwT = np.ascontiguousarray(dw.T.reshape(NIC, 128, D))
    gs = np.ascontiguousarray(
        np.asarray(L0_ffn_gate_scale, np.float32).reshape(NIC, 128).T)
    us = np.ascontiguousarray(
        np.asarray(L0_ffn_up_scale, np.float32).reshape(NIC, 128).T)
    ds = np.ascontiguousarray(
        np.asarray(L0_ffn_down_scale, np.float32).reshape(NDC, 128).T)
    a_ = np.arange(0, HD, 2, dtype=np.float32)
    inv = (np.float32(1.0) / (np.float32(1e6) ** (a_ / np.float32(HD)))
           ).astype(np.float32)
    pos = np.arange(T, dtype=np.float32)
    fr = pos[:, None] * inv[None, :]
    emb = np.concatenate([fr, fr], axis=1).astype(np.float32)
    cos_t = np.cos(emb).astype(np.float32)
    sin_t = np.sin(emb).astype(np.float32)
    ident = np.eye(128, dtype=np.float32).astype(bf)
    trim = (np.arange(128)[:, None] <= np.arange(128)[None, :]).astype(bf)

    in_maps = []
    for c in range(NCORE):
        kv = c // 2
        in_maps.append({
            "x_in": np.ascontiguousarray(x[TC * c:TC * (c + 1)]),
            "wq_in": np.ascontiguousarray(wq[256 * c:256 * (c + 1)]),
            "wk_in": np.ascontiguousarray(wk[128 * kv:128 * (kv + 1)]),
            "wv_in": np.ascontiguousarray(wv[128 * kv:128 * (kv + 1)]),
            "woT_in": woT,
            "gwT_in": gwT, "uwT_in": uwT, "dwT_in": dwT,
            "gs_in": gs, "us_in": us, "ds_in": ds,
            "cos_in": cos_t, "sin_in": sin_t,
            "ident_in": ident, "tri_in": trim,
        })
    return in_maps


def kernel(**inputs):
    in_maps = make_inputs(**inputs)
    nc, _ = _prog("full", False)
    res = run_bass_kernel_spmd(nc, in_maps, core_ids=list(range(NCORE)))
    y = np.concatenate([res.results[c]["y_out"] for c in range(NCORE)], axis=0)
    return y.reshape(1, T, D)


# version 04 (final)
# speedup vs baseline: 1.3489x; 1.3489x over previous
import numpy as np
import ml_dtypes
from contextlib import ExitStack

import concourse.bass as bass
import concourse.mybir as mybir
from concourse.bass_utils import run_bass_kernel_spmd

F32 = mybir.dt.float32
BF16 = mybir.dt.bfloat16
I32 = mybir.dt.int32
U32 = mybir.dt.uint32
ALU = mybir.AluOpType
ACT = mybir.ActivationFunctionType
AX = mybir.AxisListType.X

D = 2048
T = 8192
NCORE = 8
TC = T // NCORE
HD = 128
NDC = D // 128
NPT = TC // 128
NPTG = T // 128
IF = 8192
NIC = IF // 128
SCALE = float(HD) ** -0.5
C_EXP = 12.0
NQB = T // 512


class Trk:

    def __init__(self):
        self.w, self.r = set(), set()

    @staticmethod
    def _n(aps):
        s = set()
        for ap in aps:
            t = getattr(ap, "tensor", None)
            if t is not None:
                s.add(t.name)
            elif hasattr(ap, "name") and hasattr(ap, "shape"):
                s.add(ap.name)
        return s

    def pre(self, eng, outs, ins):
        no, ni = self._n(outs), self._n(ins)
        if (ni & self.w) or (no & (self.w | self.r)):
            eng.drain()
            self.w.clear()
            self.r.clear()
        self.w |= no
        self.r |= ni

    def clear(self):
        self.w.clear()
        self.r.clear()


def build_program(upto="full", want_taps=False):
    nc = bass.Bass("TRN2", target_bir_lowering=False, debug=False,
                   enable_partition_id=True)
    es = ExitStack()
    TV, TS = Trk(), Trk()
    LVL = ["qkv", "attn", "wo", "full"].index(upto)
    C = {"ld": 0, "lds": 0, "ldg": 0, "v": 0, "s": 0, "cc": 0, "out": 0,
         "fh": 0, "lc": 0, "lw": 0, "lx": 0, "lc1": 0, "lxs": 0, "lxr": 0,
         "lt": 0}

    din = lambda n_, sh, dt: nc.dram_tensor(n_, sh, dt, kind="ExternalInput").ap()
    dint = lambda n_, sh, dt: nc.dram_tensor(n_, sh, dt, kind="Internal").ap()

    x_in = din("x_in", [TC, D], F32)
    wq_in = din("wq_in", [256, D], F32)
    wk_in = din("wk_in", [128, D], F32)
    wv_in = din("wv_in", [128, D], F32)
    woT_in = din("woT_in", [D, D], F32)
    gwT_in = din("gwT_in", [NIC, D, 128], I32)
    uwT_in = din("uwT_in", [NIC, D, 128], I32)
    dwT_in = din("dwT_in", [NIC, 128, D], I32)
    gs_in = din("gs_in", [128, NIC], F32)
    us_in = din("us_in", [128, NIC], F32)
    ds_in = din("ds_in", [128, NDC], F32)
    cos_in = din("cos_in", [T, HD], F32)
    sin_in = din("sin_in", [T, HD], F32)
    ident_in = din("ident_in", [128, 128], BF16)
    tri_in = din("tri_in", [128, 128], BF16)

    y_out = nc.dram_tensor("y_out", [TC, D], F32, kind="ExternalOutput").ap()

    cc1_in = dint("cc1_in", [D, TC], BF16)
    cc1_out = nc.dram_tensor("cc1_out", [NCORE * D, TC], BF16, kind="Internal", addr_space="Shared").ap()
    ccs_in = dint("ccs_in", [128, NPT], F32)
    ccs_out = nc.dram_tensor("ccs_out", [NCORE * 128, NPT], F32, kind="Internal", addr_space="Shared").ap()
    cc2_in = dint("cc2_in", [T, 256], BF16)
    cc2_out = nc.dram_tensor("cc2_out", [NCORE * T, 256], BF16, kind="Internal", addr_space="Shared").ap()
    hscr = dint("hscr", [NIC, 128, 512], BF16)
    wsscr = dint("wsscr", [128, 4], F32)

    taps = {}
    if want_taps:
        tapdef = {"qx": ([128, NPT * D], BF16), "xs": ([128, NPT], F32),
                  "qkv": ([128, 3 * T + 64 * 129], BF16),
                  "attn": ([128, 64 * 256], BF16),
                  "wo": ([128, NPT * D], F32)}
        for name, (sh, dt) in tapdef.items():
            taps[name] = nc.dram_tensor("tap_" + name, sh, dt,
                                        kind="ExternalOutput").ap()

    sbt = lambda st, n_, sh, dt=F32: st.enter_context(nc.sbuf_tensor(n_, sh, dt))
    sbr = lambda st, n_, sh, dt=F32: st.enter_context(
        nc.sbuf_tensor(n_, sh, dt, side="right"))
    pst = lambda st, n_, sh, dt=F32: st.enter_context(nc.psum_tensor(n_, sh, dt))

    ident = sbt(es, "ident", [128, 128], BF16)
    tri = sbt(es, "tri", [128, 128], BF16)
    cp = sbt(es, "cp", [128, 4])
    ones_bf = sbt(es, "ones_bf", [128, 1], BF16)
    onesf_row = sbt(es, "onesf_row", [1, 128], F32)
    m128_bf = sbt(es, "m128_bf", [1, 128], BF16)
    pid_sb = sbt(es, "pid_sb", [1, 1], U32)
    xs_all = sbt(es, "xs_all", [128, NPTG])
    gs_sb = sbt(es, "gs_sb", [128, NIC])
    us_sb = sbt(es, "us_sb", [128, NIC])
    ds_sb = sbt(es, "ds_sb", [128, NDC])

    sem = lambda n_: es.enter_context(nc.semaphore(n_))
    s_ld = sem("s_ld")
    s_lc = sem("s_lc")
    s_lw = sem("s_lw")
    s_lx = sem("s_lx")
    s_lc1 = sem("s_lc1")
    s_lxs = sem("s_lxs")
    s_lxr = sem("s_lxr")
    s_lt = sem("s_lt")
    s_lda = sem("s_lda")
    s_ldb = sem("s_ldb")
    s_ga = sem("s_ga")
    s_gb = sem("s_gb")
    s_da = sem("s_da")
    s_db = sem("s_db")
    s_fla = sem("s_fla")
    s_flb = sem("s_flb")
    s_ha = sem("s_ha")
    s_hb = sem("s_hb")
    s_wsa = sem("s_wsa")
    s_wsb = sem("s_wsb")
    s_tw = sem("s_tw")
    s_lds = sem("s_lds")
    s_ldg = sem("s_ldg")
    s_v = sem("s_v")
    s_s = sem("s_s")
    s_cc = sem("s_cc")
    s_mm = sem("s_mm")
    s_rp = sem("s_rp")
    s_tp = sem("s_tp")
    s_te = sem("s_te")
    s_qb = sem("s_qb")
    s_qk = sem("s_qk")
    s_exp = sem("s_exp")
    s_msk = sem("s_msk")
    s_pv = sem("s_pv")
    s_pvd = sem("s_pvd")
    s_wo = sem("s_wo")
    s_wov = sem("s_wov")
    s_f1 = sem("s_f1")
    s_f2 = sem("s_f2")
    s_f3 = sem("s_f3")
    s_f4 = sem("s_f4")
    s_f5 = sem("s_f5")
    s_f6 = sem("s_f6")
    s_fh = sem("s_fh")
    s_out = sem("s_out")

    def dsync(eng, out, in_, cnt="ld", k=16):
        eng.dma_start(out=out, in_=in_).then_inc(
            {"ld": s_ld, "lds": s_lds, "ldg": s_ldg, "out": s_out,
             "fh": s_fh, "lc": s_lc, "lw": s_lw, "lx": s_lx, "lc1": s_lc1,
             "lxs": s_lxs, "lxr": s_lxr, "lt": s_lt}[cnt], k)
        C[cnt] += k
        return C[cnt]

    with es, nc.Block() as block:
        @block.sync
        def _(sync):
            for dst, src in ((ident, ident_in), (tri, tri_in), (gs_sb, gs_in),
                             (us_sb, us_in), (ds_sb, ds_in)):
                dsync(sync, dst[:], src[:], "lc")
            dsync(sync, pid_sb[:], nc.partition_id_tensor.ap(), "lc")
        LD_CONST = C["lc"]

        @block.vector
        def _(v):
            for j, val in enumerate((1e-5, 1.0, -C_EXP, 0.0)):
                TV.pre(v, [cp[:, j:j + 1]], [])
                v.memset(cp[:, j:j + 1], val)
            TV.pre(v, [ones_bf[:]], [])
            v.memset(ones_bf[:], 1.0)
            TV.pre(v, [onesf_row[:]], [])
            v.memset(onesf_row[:], 1.0)
            TV.pre(v, [m128_bf[:]], [])
            v.memset(m128_bf[:], -128.0)
            v.drain()
            TV.clear()
            v.sem_inc(s_v, 1)
            C["v"] += 1
        V_CONST = C["v"]

        wt_es = ExitStack()
        wT_all = wt_es.enter_context(nc.sbuf_tensor("wT_all", [128, NDC * 512], BF16))
        wsrep_es = ExitStack()
        ws_rep = wsrep_es.enter_context(nc.sbuf_tensor("ws_rep", [128, 512], F32))
        pw_es = ExitStack()
        pw = [pst(pw_es, f"pw{i}", [128, 128], BF16) for i in range(4)]
        NTR = [0]

        w_es = ExitStack()
        w_f = sbt(w_es, "w_f", [128, 4 * D])
        wdq_bf = sbt(w_es, "wdq_bf", [128, 4 * D], BF16)
        wscr = sbt(w_es, "wscr", [128, D])
        wscri = sbt(w_es, "wscri", [128, D], I32)
        colsc = sbt(w_es, "colsc", [128, 16])

        @block.sync
        def _(sync):
            dsync(sync, w_f[:, 0:2 * D].rearrange("p (t d) -> p t d", t=2),
                  wq_in.rearrange("(t p) d -> p t d", p=128), "lw")
            dsync(sync, w_f[:, 2 * D:3 * D], wk_in[:], "lw")
            dsync(sync, w_f[:, 3 * D:4 * D], wv_in[:], "lw")
        LD_W = C["lw"]

        @block.vector
        def _(v):
            v.wait_ge(s_lw, LD_W)
            TV.clear()
            for i in range(4):
                w_t = w_f[:, i * D:(i + 1) * D]
                TV.pre(v, [colsc[:, i:i + 1]], [w_t])
                v.reduce_max(colsc[:, i:i + 1].rearrange("p (a o) -> p a o", a=1),
                             w_t.rearrange("p (a d) -> p a d", a=1), axis=AX,
                             apply_absolute_value=True)
            TV.pre(v, [colsc[:, 4:8]], [colsc[:, 0:4]])
            v.tensor_scalar(colsc[:, 4:8], colsc[:, 0:4], 1.0 / 127.0, 1e-8,
                            ALU.mult, ALU.max)
            TV.pre(v, [colsc[:, 8:12]], [colsc[:, 4:8]])
            v.reciprocal(colsc[:, 8:12], colsc[:, 4:8])
            for i in range(4):
                w_t = w_f[:, i * D:(i + 1) * D]
                TV.pre(v, [wscr[:]], [w_t, colsc[:, 8 + i:9 + i]])
                v.tensor_scalar(wscr[:], w_t, colsc[:, 8 + i:9 + i], None, ALU.mult)
                TV.pre(v, [wscr[:]], [wscr[:]])
                v.tensor_scalar(wscr[:], wscr[:], 12582912.0, 12582912.0,
                                ALU.add, ALU.subtract)
                TV.pre(v, [wscri[:]], [wscr[:]])
                v.tensor_copy(wscri[:], wscr[:])
                TV.pre(v, [wdq_bf[:, i * D:(i + 1) * D]], [wscri[:]])
                v.tensor_copy(wdq_bf[:, i * D:(i + 1) * D], wscri[:])
            v.drain()
            TV.clear()
            v.sem_inc(s_v, 1)
            C["v"] += 1
        V_WQ = C["v"]

        ws_row = w_es.enter_context(nc.sbuf_tensor("ws_row", [1, 512], F32))
        pwf = w_es.enter_context(nc.psum_tensor("pwf", [128, 512], F32))

        @block.sync
        def _(sync):
            sync.wait_ge(s_v, V_WQ)
            sync.dma_start(out=wsscr[:], in_=colsc[:, 4:8]).then_inc(s_wsa, 16)
            sync.wait_ge(s_wsa, 16)
            sync.dma_start(
                out=ws_row[0:1, :].rearrange("a (i p) -> a i p", i=4),
                in_=wsscr.rearrange("(a p) i -> a i p", a=1),
                allow_slow_non_contiguous=True).then_inc(s_wsb, 16)

        @block.tensor
        def _(t):
            t.wait_ge(s_wsb, 16)
            t.wait_ge(s_v, V_CONST)
            t.matmul(pwf[:], onesf_row[0:1, :], ws_row[0:1, :],
                     start=True, stop=True).then_inc(s_tw, 1)

        @block.vector
        def _(v):
            v.wait_ge(s_tw, 1)
            TV.clear()
            TV.pre(v, [ws_rep[:]], [pwf])
            v.tensor_copy(ws_rep[:], pwf[:])
            v.drain()
            TV.clear()
            v.sem_inc(s_v, 1)
            C["v"] += 1
        V_WSREP = C["v"]

        def emit_transposes(items, wait_v, psums):
            base = NTR[0]
            n_items = len(items)

            nb = len(psums)

            @block.tensor
            def _(t):
                t.wait_ge(s_v, wait_v)
                t.wait_ge(s_lc, LD_CONST)
                for k in range(n_items):
                    m = base + k
                    if m >= nb:
                        t.wait_ge(s_te, m - nb + 1)
                    t.transpose(psums[m % nb][:], items[k][0],
                                ident[:]).then_inc(s_tp, 1)

            @block.vector
            def _(v):
                for k in range(n_items):
                    m = base + k
                    v.wait_ge(s_tp, m + 1)
                    TV.clear()
                    TV.pre(v, [items[k][1]], [])
                    v.tensor_copy(items[k][1], psums[m % nb][:]).then_inc(s_te, 1)

            NTR[0] += n_items

        wtr = []
        for i in range(4):
            for dc in range(NDC):
                wtr.append((wdq_bf[:, i * D + dc * 128:i * D + dc * 128 + 128],
                            wT_all[:, dc * 512 + i * 128:dc * 512 + i * 128 + 128]))
        emit_transposes(wtr, V_WQ, pw)

        @block.vector
        def _(v):
            v.wait_ge(s_te, NTR[0])
            TV.clear()
            v.sem_inc(s_v, 1)
            C["v"] += 1
        V_WT = C["v"]
        w_es.close()

        a_es = ExitStack()
        xa = sbt(a_es, "xa", [128, NPT * D])
        qx_sb = sbt(a_es, "qx_sb", [128, NPT * D], BF16)
        qxT_own = sbt(a_es, "qxT_own", [128, NDC * TC], BF16)
        scr_a = sbt(a_es, "scr_a", [128, D])
        scr_i = sbt(a_es, "scr_i", [128, D], I32)
        lncol = sbt(a_es, "lncol", [128, 40])
        xs_own = sbt(a_es, "xs_own", [128, NPT])

        @block.sync
        def _(sync):
            sync.wait_ge(s_v, V_WT)
            dsync(sync, xa[:].rearrange("p (t d) -> p t d", t=NPT),
                  x_in.rearrange("(t p) d -> p t d", p=128), "lx")
        LD_X = C["lx"]

        @block.vector
        def _(v):
            v.wait_ge(s_lx, LD_X)
            TV.clear()
            for p in range(NPT):
                x_t = xa[:, p * D:(p + 1) * D]
                TV.pre(v, [scr_a[:], lncol[:, 8 + p:9 + p]], [x_t])
                v.scalar_tensor_tensor(scr_a[:], x_t, 1.0, x_t, ALU.bypass,
                                       ALU.mult, accum_out=lncol[:, 8 + p:9 + p])
                TV.pre(v, [lncol[:, p:p + 1]], [x_t])
                v.reduce_sum(lncol[:, p:p + 1].rearrange("p (a o) -> p a o", a=1),
                             x_t.rearrange("p (a d) -> p a d", a=1), axis=AX)
                TV.pre(v, [lncol[:, 16 + p:17 + p]], [x_t])
                v.reduce_max(lncol[:, 16 + p:17 + p].rearrange("p (a o) -> p a o", a=1),
                             x_t.rearrange("p (a d) -> p a d", a=1), axis=AX)
                TV.pre(v, [lncol[:, 24 + p:25 + p]], [x_t])
                v.tensor_reduce(lncol[:, 24 + p:25 + p].rearrange("p (a o) -> p a o", a=1),
                                x_t.rearrange("p (a d) -> p a d", a=1), AX, ALU.min)
            TV.pre(v, [lncol[:, 0:8]], [lncol[:, 0:8]])
            v.tensor_scalar(lncol[:, 0:8], lncol[:, 0:8], 1.0 / D, None, ALU.mult)
            TV.pre(v, [lncol[:, 32:40]], [lncol[:, 0:8]])
            v.tensor_tensor(lncol[:, 32:40], lncol[:, 0:8], lncol[:, 0:8], ALU.mult)
            TV.pre(v, [lncol[:, 8:16]], [lncol[:, 8:16], lncol[:, 32:40]])
            v.scalar_tensor_tensor(lncol[:, 8:16], lncol[:, 8:16], 1.0 / D,
                                   lncol[:, 32:40], ALU.mult, ALU.subtract)
            TV.pre(v, [lncol[:, 8:16]], [lncol[:, 8:16]])
            v.tensor_scalar(lncol[:, 8:16], lncol[:, 8:16], 1e-5, None, ALU.add)
            TV.pre(v, [lncol[:, 16:24]], [lncol[:, 16:24], lncol[:, 0:8]])
            v.tensor_tensor(lncol[:, 16:24], lncol[:, 16:24], lncol[:, 0:8],
                            ALU.subtract)
            TV.pre(v, [lncol[:, 24:32]], [lncol[:, 0:8], lncol[:, 24:32]])
            v.tensor_tensor(lncol[:, 24:32], lncol[:, 0:8], lncol[:, 24:32],
                            ALU.subtract)
            TV.pre(v, [lncol[:, 16:24]], [lncol[:, 16:24], lncol[:, 24:32]])
            v.tensor_tensor(lncol[:, 16:24], lncol[:, 16:24], lncol[:, 24:32],
                            ALU.max)
            v.drain()
            TV.clear()
            v.sem_inc(s_v, 1)
            C["v"] += 1
        V_LN1 = C["v"]

        @block.scalar
        def _(s):
            s.wait_ge(s_v, V_CONST)
            s.wait_ge(s_v, V_LN1)
            TS.clear()
            TS.pre(s, [lncol[:, 32:40]], [lncol[:, 8:16], cp])
            s.activation(lncol[:, 32:40], lncol[:, 8:16], ACT.Sqrt,
                         bias=cp[:, 3:4], scale=cp[:, 1:2])
            s.drain()
            TS.clear()
            s.sem_inc(s_s, 1)
            C["s"] += 1
        S_LN1 = C["s"]

        @block.vector
        def _(v):
            v.wait_ge(s_s, S_LN1)
            TV.clear()
            TV.pre(v, [lncol[:, 24:32]], [lncol[:, 32:40]])
            v.reciprocal(lncol[:, 24:32], lncol[:, 32:40])
            TV.pre(v, [xs_own[:]], [lncol[:, 16:24], lncol[:, 24:32]])
            v.tensor_tensor(xs_own[:], lncol[:, 16:24], lncol[:, 24:32], ALU.mult)
            TV.pre(v, [xs_own[:]], [xs_own[:]])
            v.tensor_scalar(xs_own[:], xs_own[:], 1.0 / 127.0, 1e-8,
                            ALU.mult, ALU.max)
            TV.pre(v, [lncol[:, 32:40]], [lncol[:, 16:24]])
            v.reciprocal(lncol[:, 32:40], lncol[:, 16:24])
            TV.pre(v, [lncol[:, 32:40]], [lncol[:, 32:40]])
            v.tensor_scalar(lncol[:, 32:40], lncol[:, 32:40], 127.0, None, ALU.mult)
            TV.pre(v, [lncol[:, 16:24]], [lncol[:, 0:8], lncol[:, 32:40]])
            v.tensor_tensor(lncol[:, 16:24], lncol[:, 0:8], lncol[:, 32:40],
                            ALU.mult)
            for p in range(NPT):
                x_t = xa[:, p * D:(p + 1) * D]
                TV.pre(v, [scr_a[:]],
                       [x_t, lncol[:, 32 + p:33 + p], lncol[:, 16 + p:17 + p]])
                v.tensor_scalar(scr_a[:], x_t, lncol[:, 32 + p:33 + p],
                                lncol[:, 16 + p:17 + p], ALU.mult, ALU.subtract)
                TV.pre(v, [scr_a[:]], [scr_a[:]])
                v.tensor_scalar(scr_a[:], scr_a[:], 12582912.0, 12582912.0,
                                ALU.add, ALU.subtract)
                TV.pre(v, [scr_i[:]], [scr_a[:]])
                v.tensor_copy(scr_i[:], scr_a[:])
                TV.pre(v, [qx_sb[:, p * D:(p + 1) * D]], [scr_i[:]])
                v.tensor_copy(qx_sb[:, p * D:(p + 1) * D], scr_i[:])
            v.drain()
            TV.clear()
            v.sem_inc(s_v, 1)
            C["v"] += 1
        V_QX = C["v"]

        qtr = []
        for p in range(NPT):
            for dc in range(NDC):
                qtr.append((qx_sb[:, p * D + dc * 128:p * D + dc * 128 + 128],
                            qxT_own[:, dc * TC + p * 128:dc * TC + p * 128 + 128]))
        emit_transposes(qtr, V_QX, pw)

        @block.vector
        def _(v):
            v.wait_ge(s_te, NTR[0])
            TV.clear()
            v.sem_inc(s_v, 1)
            C["v"] += 1
        V_QXT = C["v"]

        @block.sync
        def _(sync):
            sync.wait_ge(s_v, V_QXT)
            dsync(sync, cc1_in.rearrange("(dc p) t -> p dc t", p=128),
                  qxT_own[:].rearrange("p (dc t) -> p dc t", dc=NDC), "lc1")
            dsync(sync, ccs_in[:], xs_own[:], "lc1")
            if want_taps:
                dsync(sync, taps["qx"][:], qx_sb[:], "lc1")
                dsync(sync, taps["xs"][:], xs_own[:], "lc1")
        LD_CC1 = C["lc1"]

        @block.gpsimd
        def _(g):
            g.wait_ge(s_lc1, LD_CC1)
            g.collective_compute(
                "AllGather", ALU.bypass, replica_groups=[list(range(NCORE))],
                ins=[cc1_in[:]], outs=[cc1_out[:]]).then_inc(s_cc, 1)
            g.collective_compute(
                "AllGather", ALU.bypass, replica_groups=[list(range(NCORE))],
                ins=[ccs_in[:]], outs=[ccs_out[:]]).then_inc(s_cc, 1)
            C["cc"] += 2
        CC1 = C["cc"]

        @block.sync
        def _(sync):
            sync.wait_ge(s_cc, CC1)
            dsync(sync, xs_all[:].rearrange("p (r q) -> p r q", r=NCORE),
                  ccs_out.rearrange("(r p) q -> p r q", p=128), "lxs")
        LD_XS = C["lxs"]
        a_es.close()

        qkv_es = ExitStack()
        qT_sb = sbr(qkv_es, "qT_sb", [128, 2 * T], BF16)
        kT_sb = sbr(qkv_es, "kT_sb", [128, T], BF16)
        v_sb = sbr(qkv_es, "v_sb", [128, 64 * 129], BF16)

        b_es = ExitStack()
        rk_buf = sbt(b_es, "rk_buf", [128, 2 * NDC * TC], BF16)
        cs_buf = sbt(b_es, "cs_buf", [128, 2 * 2048])
        rp_f = sbt(b_es, "rp_f", [128, 512])
        rp_dq = sbt(b_es, "rp_dq", [128, 512])
        rp_t1 = sbt(b_es, "rp_t1", [128, 192])
        rp_t2 = sbt(b_es, "rp_t2", [128, 192])
        qv_i = sbt(b_es, "qv_i", [128, 512], I32)
        qv_a = sbt(b_es, "qv_a", [128, 512])
        qkv_bf = sbt(b_es, "qkv_bf", [128, 2 * 512], BF16)
        qcols = sbt(b_es, "qcols", [128, 16])
        pqkv = [pst(b_es, f"pqkv{i}", [128, 512]) for i in range(2)]

        @block.vector
        def _(v):
            v.wait_ge(s_cc, CC1)
            TV.clear()
            vo = v_sb[:].rearrange("p (c w) -> p c w", w=129)[:, :, 128:129]
            TV.pre(v, [vo], [])
            v.memset(vo, 1.0)
            v.drain()
            TV.clear()
            v.sem_inc(s_v, 1)
            C["v"] += 1
        V_VONES = C["v"]

        LD_B0 = C["ld"]

        @block.sync
        def _(sync):
            for r in range(NCORE):
                half = r % 2
                sb_sem = s_lda if half == 0 else s_ldb
                sync.wait_ge(s_cc, CC1)
                if r >= 2:
                    sync.wait_ge(s_qb, (r - 1) * NPT)
                sync.dma_start(
                    out=rk_buf[:, half * NDC * TC:(half + 1) * NDC * TC]
                    .rearrange("p (dc t) -> p dc t", dc=NDC),
                    in_=cc1_out[r * D:(r + 1) * D, :]
                    .rearrange("(dc p) t -> p dc t", p=128)).then_inc(sb_sem, 16)
                sync.dma_start(
                    out=cs_buf[:, half * 2048:half * 2048 + 1024]
                    .rearrange("p (q d) -> p q d", q=NPT),
                    in_=cos_in[r * TC:(r + 1) * TC, :]
                    .rearrange("(q p) d -> p q d", p=128)).then_inc(sb_sem, 16)
                sync.dma_start(
                    out=cs_buf[:, half * 2048 + 1024:half * 2048 + 2048]
                    .rearrange("p (q d) -> p q d", q=NPT),
                    in_=sin_in[r * TC:(r + 1) * TC, :]
                    .rearrange("(q p) d -> p q d", p=128)).then_inc(sb_sem, 16)
        LD_RK = lambda r: 48 * (r // 2 + 1)

        base_b = NTR[0]

        @block.tensor
        def _(t):
            t.wait_ge(s_lxs, LD_XS)
            t.wait_ge(s_v, V_QXT)
            for g in range(NPTG + 1):
                if g < NPTG:
                    r, pt = divmod(g, NPT)
                    half, bank = r % 2, g % 2
                    if pt == 0:
                        t.wait_ge(s_lda if r % 2 == 0 else s_ldb, LD_RK(r))
                    if g >= 2:
                        t.wait_ge(s_rp, g - 1)
                    for dc in range(NDC):
                        mm = t.matmul(
                            pqkv[bank][:],
                            rk_buf[:, half * NDC * TC + dc * TC + pt * 128:
                                   half * NDC * TC + dc * TC + pt * 128 + 128],
                            wT_all[:, dc * 512:(dc + 1) * 512],
                            start=(dc == 0), stop=(dc == NDC - 1))
                    mm.then_inc(s_mm, 1)
                if g >= 1:
                    tq = g - 1
                    t.wait_ge(s_qb, tq + 1)
                    for hh in range(3):
                        m = base_b + tq * 3 + hh
                        if m >= 4:
                            t.wait_ge(s_te, m - 3)
                        t.transpose(pw[m % 4][:],
                                    qkv_bf[:, (tq % 2) * 512 + hh * 128:
                                           (tq % 2) * 512 + hh * 128 + 128],
                                    ident[:]).then_inc(s_tp, 1)

        @block.vector
        def _(v):
            v.wait_ge(s_v, V_VONES)
            for g in range(NPTG + 1):
                if g < NPTG:
                    r, pt = divmod(g, NPT)
                    half, bank = r % 2, g % 2
                    v.wait_ge(s_mm, g + 1)
                    TV.clear()
                    TV.pre(v, [rp_dq[:]], [pqkv[bank], ws_rep])
                    v.tensor_tensor(rp_dq[:], pqkv[bank][:], ws_rep[:], ALU.mult)
                    v.drain()
                    TV.clear()
                    v.sem_inc(s_rp, 1)
                    W = rp_dq[:]
                    W3 = W.rearrange("p (h d) -> p h d", h=4)
                    cosb = cs_buf[:, half * 2048 + pt * 128:
                                  half * 2048 + pt * 128 + 128]
                    sinb = cs_buf[:, half * 2048 + 1024 + pt * 128:
                                  half * 2048 + 1024 + pt * 128 + 128]
                    bc = lambda a, w: a.rearrange("p (o d) -> p o d", o=1)\
                        .broadcast_to((128, 3, w))
                    t13 = rp_t1[:].rearrange("p (h d) -> p h d", h=3)
                    t23 = rp_t2[:].rearrange("p (h d) -> p h d", h=3)
                    rp3 = rp_f[:].rearrange("p (h d) -> p h d", h=4)
                    TV.pre(v, [rp_t1[:]], [])
                    v.tensor_tensor(t13, W3[:, 0:3, 0:64], bc(cosb[:, 0:64], 64),
                                    ALU.mult)
                    TV.pre(v, [rp_t2[:]], [])
                    v.tensor_tensor(t23, W3[:, 0:3, 64:128], bc(sinb[:, 0:64], 64),
                                    ALU.mult)
                    TV.pre(v, [rp_f[:]], [rp_t1[:], rp_t2[:]])
                    v.tensor_tensor(rp3[:, 0:3, 0:64], t13, t23, ALU.subtract)
                    TV.pre(v, [rp_t1[:]], [rp_f[:]])
                    v.tensor_tensor(t13, W3[:, 0:3, 64:128], bc(cosb[:, 64:128], 64),
                                    ALU.mult)
                    TV.pre(v, [rp_t2[:]], [rp_t1[:]])
                    v.tensor_tensor(t23, W3[:, 0:3, 0:64], bc(sinb[:, 64:128], 64),
                                    ALU.mult)
                    TV.pre(v, [rp_f[:]], [rp_t1[:], rp_t2[:]])
                    v.tensor_tensor(rp3[:, 0:3, 64:128], t13, t23, ALU.add)
                    TV.pre(v, [rp_f[:]], [rp_dq[:]])
                    v.tensor_copy(rp_f[:, 384:512], W[:, 384:512])
                    TV.pre(v, [qcols[:, 0:4]], [rp_f[:]])
                    v.reduce_max(qcols[:, 0:4].rearrange("p (h o) -> p h o", o=1),
                                 rp_f[:].rearrange("p (h d) -> p h d", h=4),
                                 axis=AX, apply_absolute_value=True)
                    xsb = xs_all[:, g:g + 1].rearrange("p (a o) -> p a o", a=1)\
                        .broadcast_to((128, 4, 1))
                    TV.pre(v, [qcols[:, 4:8]], [qcols[:, 0:4], xs_all])
                    v.tensor_tensor(qcols[:, 4:8].rearrange("p (h o) -> p h o", o=1),
                                    qcols[:, 0:4].rearrange("p (h o) -> p h o", o=1),
                                    xsb, ALU.mult)
                    TV.pre(v, [qcols[:, 4:8]], [qcols[:, 4:8]])
                    v.tensor_scalar(qcols[:, 4:8], qcols[:, 4:8], 1.0 / 127.0,
                                    1e-8, ALU.mult, ALU.max)
                    TV.pre(v, [qcols[:, 8:10]], [qcols[:, 4:6]])
                    v.tensor_scalar(qcols[:, 8:10], qcols[:, 4:6], SCALE, None,
                                    ALU.mult)
                    TV.pre(v, [qcols[:, 10:12]], [qcols[:, 6:8]])
                    v.tensor_copy(qcols[:, 10:12], qcols[:, 6:8])
                    TV.pre(v, [qcols[:, 12:16]], [qcols[:, 0:4]])
                    v.reciprocal(qcols[:, 12:16], qcols[:, 0:4])
                    TV.pre(v, [qcols[:, 12:16]], [qcols[:, 12:16]])
                    v.tensor_scalar(qcols[:, 12:16], qcols[:, 12:16], 127.0, None,
                                    ALU.mult)
                    for h4 in range(4):
                        TV.pre(v, [qv_a[:, h4 * 128:(h4 + 1) * 128]],
                               [rp_f[:], qcols[:, 12 + h4:13 + h4]])
                        v.tensor_scalar(qv_a[:, h4 * 128:(h4 + 1) * 128],
                                        rp_f[:, h4 * 128:(h4 + 1) * 128],
                                        qcols[:, 12 + h4:13 + h4], None, ALU.mult)
                    TV.pre(v, [qv_a[:]], [qv_a[:]])
                    v.tensor_scalar(qv_a[:], qv_a[:], 12582912.0, 12582912.0,
                                    ALU.add, ALU.subtract)
                    TV.pre(v, [qv_i[:]], [qv_a[:]])
                    v.tensor_copy(qv_i[:], qv_a[:])
                    if g >= 2:
                        v.wait_ge(s_tp, base_b + 3 * (g - 1))
                    for h4 in range(4):
                        dst = qkv_bf[:, bank * 512 + h4 * 128:
                                     bank * 512 + (h4 + 1) * 128]
                        TV.pre(v, [dst], [qv_i[:], qcols[:, 8 + h4:9 + h4]])
                        v.tensor_scalar(dst, qv_i[:, h4 * 128:(h4 + 1) * 128],
                                        qcols[:, 8 + h4:9 + h4], None, ALU.mult)
                    dstv = v_sb[:, g * 129:g * 129 + 128]
                    TV.pre(v, [dstv], [qkv_bf])
                    v.tensor_copy(dstv, qkv_bf[:, bank * 512 + 384:
                                               bank * 512 + 512])
                    v.drain()
                    TV.clear()
                    v.sem_inc(s_qb, 1)
                if g >= 1:
                    tq = g - 1
                    for hh in range(3):
                        m = base_b + tq * 3 + hh
                        v.wait_ge(s_tp, m + 1)
                        TV.clear()
                        if hh < 2:
                            dst = qT_sb[:, hh * T + tq * 128:hh * T + tq * 128 + 128]
                        else:
                            dst = kT_sb[:, tq * 128:tq * 128 + 128]
                        TV.pre(v, [dst], [])
                        v.tensor_copy(dst, pw[m % 4][:]).then_inc(s_te, 1)
            v.drain()
            TV.clear()
            v.sem_inc(s_v, 1)
            C["v"] += 1
        V_QKV = C["v"]
        NTR[0] = base_b + NPTG * 3

        if want_taps:
            @block.sync
            def _(sync):
                sync.wait_ge(s_v, V_QKV)
                tp = taps["qkv"]
                dsync(sync, tp[:, 0:2 * T], qT_sb[:], "lt")
                dsync(sync, tp[:, 2 * T:3 * T], kT_sb[:], "lt")
                dsync(sync, tp[:, 3 * T:3 * T + 64 * 129], v_sb[:], "lt")

        def finish(extra=()):
            tot = [(s_ld, C["ld"]), (s_lds, C["lds"]), (s_ldg, C["ldg"]),
                   (s_out, C["out"]), (s_lt, C["lt"]), (s_lc1, C["lc1"]),
                   (s_fh, C["fh"])]

            @block.gpsimd
            def _(g):
                for semh, val in extra:
                    g.wait_ge(semh, val)
                for semh, val in tot:
                    if val:
                        g.wait_ge(semh, val)

        if LVL == 0:
            finish([(s_v, V_QKV)])
            b_es.close()
            pw_es.close()
            wsrep_es.close()
            wt_es.close()
            qkv_es.close()
            return nc, taps

        b_es.close()
        pw_es.close()
        wsrep_es.close()
        wt_es.close()

        c_es = ExitStack()
        pT = [sbr(c_es, f"pT{i}", [128, 1024], BF16) for i in range(2)]
        stp = [pst(c_es, f"st{i}", [128, 1024]) for i in range(2)]
        pv_ps = [pst(c_es, f"pv{j}", [128, 512]) for j in range(4)]
        rcol = sbr(c_es, "rcol", [128, 4])
        attn_sb = c_es.enter_context(
            nc.sbuf_tensor("attn_sb", [128, 64 * 256], BF16, side="right"))

        halves = []
        for qb in range(NQB):
            for h in range(2):
                for kb in range(qb + 1):
                    for hf in range(2):
                        halves.append((qb, h, kb, hf, kb == qb))
        NHALF = len(halves)
        grp_end = {}
        nmsk = [0] * (NHALF + 1)
        ngrp = [0] * (NHALF + 1)
        for n, (qb, h, kb, hf, diag) in enumerate(halves):
            nmsk[n + 1] = nmsk[n] + (2 if diag else 0)
            ngrp[n + 1] = ngrp[n]
            if diag and hf == 1:
                grp_end[n] = (qb, h)
                ngrp[n + 1] += 1

        @block.tensor
        def _(t):
            t.wait_ge(s_v, V_QKV)

            def emit_qk(n):
                qb, h, kb, hf, diag = halves[n]
                if n >= 2:
                    t.wait_ge(s_exp, n - 1)
                for ci in range(2):
                    i = 2 * hf + ci
                    mm = t.matmul(
                        stp[hf][:, ci * 512:(ci + 1) * 512],
                        kT_sb[:, kb * 512 + i * 128:kb * 512 + i * 128 + 128],
                        qT_sb[:, h * T + qb * 512:h * T + qb * 512 + 512],
                        start=True, stop=True)
                mm.then_inc(s_qk, 1)

            def emit_pv(n):
                qb, h, kb, hf, diag = halves[n]
                t.wait_ge(s_exp, n + 1)
                if diag:
                    t.wait_ge(s_msk, nmsk[n + 1])
                if kb == 0 and hf == 0:
                    t.wait_ge(s_pvd, ngrp[n])
                last = None
                for ci in range(2):
                    i = 2 * hf + ci
                    for j in range(4):
                        if diag and j < i:
                            continue
                        last = t.matmul(
                            pv_ps[j][:, 0:129],
                            pT[hf][:, ci * 512 + j * 128:ci * 512 + j * 128 + 128],
                            v_sb[:, (kb * 4 + i) * 129:(kb * 4 + i) * 129 + 129],
                            start=(kb == 0 and i == 0),
                            stop=(diag and i == j))
                last.then_inc(s_pv, 1)

            emit_qk(0)
            for n in range(NHALF):
                if n + 1 < NHALF:
                    emit_qk(n + 1)
                emit_pv(n)

        @block.scalar
        def _(s):
            s.wait_ge(s_v, V_CONST)
            for n in range(NHALF):
                s.wait_ge(s_qk, n + 1)
                if n >= 2:
                    s.wait_ge(s_pv, n - 1)
                TS.clear()
                TS.pre(s, [pT[n % 2][:]], [stp[n % 2][:], cp])
                s.activation(pT[n % 2][:], stp[n % 2][:], ACT.Exp,
                             bias=cp[:, 2:3], scale=cp[:, 1:2]).then_inc(s_exp, 1)

        @block.vector
        def _(v):
            for n, (qb, h, kb, hf, diag) in enumerate(halves):
                if diag:
                    v.wait_ge(s_exp, n + 1)
                    TV.clear()
                    for ci in range(2):
                        i = 2 * hf + ci
                        blk = pT[hf][:, ci * 512 + i * 128:ci * 512 + i * 128 + 128]
                        TV.pre(v, [blk], [blk, tri])
                        v.tensor_tensor(blk, blk, tri[:], ALU.mult).then_inc(s_msk, 1)
                if n in grp_end:
                    v.wait_ge(s_pv, n + 1)
                    TV.clear()
                    for j in range(4):
                        TV.pre(v, [rcol[:, j:j + 1]], [pv_ps[j]])
                        v.reciprocal(rcol[:, j:j + 1], pv_ps[j][:, 128:129])
                        dst = attn_sb[:, (4 * qb + j) * 256 + h * 128:
                                      (4 * qb + j) * 256 + h * 128 + 128]
                        TV.pre(v, [dst], [pv_ps[j], rcol[:, j:j + 1]])
                        v.tensor_scalar(dst, pv_ps[j][:, 0:128],
                                        rcol[:, j:j + 1], None, ALU.mult)
                    v.drain()
                    TV.clear()
                    v.sem_inc(s_pvd, 1)
            v.sem_inc(s_v, 1)
            C["v"] += 1
        V_ATTN = C["v"]

        if want_taps:
            @block.sync
            def _(sync):
                sync.wait_ge(s_v, V_ATTN)
                dsync(sync, taps["attn"][:], attn_sb[:], "lt")
        LT_ATTN = C["lt"]

        if LVL == 1:
            finish([(s_v, V_ATTN)])
            c_es.close()
            qkv_es.close()
            return nc, taps

        c_es.close()
        qkv_es.close()
        x_res = es.enter_context(nc.sbuf_tensor("x_res", [128, NPT * D], F32))
        d_es = ExitStack()
        attnT = sbt(d_es, "attnT", [128, NDC * TC], BF16)
        woT_bf = sbt(d_es, "woT_bf", [128, NDC * D], BF16)
        pwo = [pst(d_es, f"pwo{i}", [128, 512]) for i in range(2)]

        @block.gpsimd
        def _(g):
            g.wait_ge(s_v, V_ATTN)
            dsync(g, cc2_in.rearrange("(c p) o -> p c o", p=128),
                  attn_sb[:].rearrange("p (c o) -> p c o", c=64), "ldg")
            g.wait_ge(s_ldg, C["ldg"])
            if want_taps:
                g.wait_ge(s_lt, LT_ATTN)
            for q in range(4):
                dsync(g, woT_bf[:, q * 4 * D:(q + 1) * 4 * D]
                      .rearrange("p (hc o) -> p hc o", hc=4),
                      woT_in[q * 512:(q + 1) * 512, :]
                      .rearrange("(hc p) o -> p hc o", p=128), "ldg")
            g.collective_compute(
                "AllGather", ALU.bypass, replica_groups=[list(range(NCORE))],
                ins=[cc2_in[:]], outs=[cc2_out[:]]).then_inc(s_cc, 1)
            C["cc"] += 1
        CC2 = C["cc"]
        LDG_WO = C["ldg"]

        @block.sync
        def _(sync):
            sync.wait_ge(s_v, V_ATTN)
            dsync(sync, x_res[:].rearrange("p (t d) -> p t d", t=NPT),
                  x_in.rearrange("(t p) d -> p t d", p=128), "lxr")
        LD_XRES = C["lxr"]

        @block.scalar
        def _(s):
            s.wait_ge(s_cc, CC2)
            s.wait_ge(s_lc, LD_CONST)
            TS.clear()
            reg = s.value_load(pid_sb[0:1, 0:1])
            for arm in s.Switch(reg, NCORE):
                for hg in range(16):
                    r, hh = divmod(hg, 2)
                    s.dma_start_transpose(
                        attnT[:, hg * TC:(hg + 1) * TC],
                        cc2_out[r * T + arm * TC:r * T + (arm + 1) * TC,
                                hh * 128:(hh + 1) * 128]).then_inc(s_lds, 16)
            C["lds"] += 16 * 16
        LDS_AT = C["lds"]

        @block.tensor
        def _(t):
            t.wait_ge(s_lds, LDS_AT)
            t.wait_ge(s_ldg, LDG_WO)
            for p in range(NPT):
                for oc in range(4):
                    u = p * 4 + oc
                    if u >= 2:
                        t.wait_ge(s_wov, u - 1)
                    for hg in range(16):
                        mm = t.matmul(
                            pwo[u % 2][:],
                            attnT[:, hg * TC + p * 128:hg * TC + p * 128 + 128],
                            woT_bf[:, hg * D + oc * 512:hg * D + oc * 512 + 512],
                            start=(hg == 0), stop=(hg == 15))
                    mm.then_inc(s_wo, 1)

        @block.vector
        def _(v):
            v.wait_ge(s_lxr, LD_XRES)
            for p in range(NPT):
                for oc in range(4):
                    u = p * 4 + oc
                    v.wait_ge(s_wo, u + 1)
                    TV.clear()
                    xsl = x_res[:, p * D + oc * 512:p * D + oc * 512 + 512]
                    TV.pre(v, [xsl], [xsl, pwo[u % 2]])
                    v.tensor_tensor(xsl, xsl, pwo[u % 2][:], ALU.add
                                    ).then_inc(s_wov, 1)
            v.drain()
            TV.clear()
            v.sem_inc(s_v, 1)
            C["v"] += 1
        V_WO = C["v"]

        if want_taps:
            @block.sync
            def _(sync):
                sync.wait_ge(s_v, V_WO)
                dsync(sync, taps["wo"][:], x_res[:], "lt")
        LT_WO = C["lt"]

        if LVL == 2:
            finish([(s_v, V_WO)])
            d_es.close()
            return nc, taps

        d_es.close()

        e1 = ExitStack()
        xln_bf = sbt(e1, "xln_bf", [128, NPT * D], BF16)
        escr = sbt(e1, "escr", [128, D])
        elncol = sbt(e1, "elncol", [128, 40])
        e2 = ExitStack()
        xlnT = sbr(e2, "xlnT", [128, NDC * TC], BF16)
        epw_es = ExitStack()
        epw = [pst(epw_es, f"epw{i}", [128, 128], BF16) for i in range(2)]

        @block.vector
        def _(v):
            v.wait_ge(s_v, V_WO)
            if want_taps:
                v.wait_ge(s_lt, LT_WO)
            TV.clear()
            for p in range(NPT):
                x_t = x_res[:, p * D:(p + 1) * D]
                TV.pre(v, [escr[:], elncol[:, 8 + p:9 + p]], [x_t])
                v.scalar_tensor_tensor(escr[:], x_t, 1.0, x_t, ALU.bypass,
                                       ALU.mult, accum_out=elncol[:, 8 + p:9 + p])
                TV.pre(v, [elncol[:, p:p + 1]], [x_t])
                v.reduce_sum(elncol[:, p:p + 1].rearrange("p (a o) -> p a o", a=1),
                             x_t.rearrange("p (a d) -> p a d", a=1), axis=AX)
            TV.pre(v, [elncol[:, 0:8]], [elncol[:, 0:8]])
            v.tensor_scalar(elncol[:, 0:8], elncol[:, 0:8], 1.0 / D, None, ALU.mult)
            TV.pre(v, [elncol[:, 32:40]], [elncol[:, 0:8]])
            v.tensor_tensor(elncol[:, 32:40], elncol[:, 0:8], elncol[:, 0:8],
                            ALU.mult)
            TV.pre(v, [elncol[:, 8:16]], [elncol[:, 8:16], elncol[:, 32:40]])
            v.scalar_tensor_tensor(elncol[:, 8:16], elncol[:, 8:16], 1.0 / D,
                                   elncol[:, 32:40], ALU.mult, ALU.subtract)
            TV.pre(v, [elncol[:, 8:16]], [elncol[:, 8:16]])
            v.tensor_scalar(elncol[:, 8:16], elncol[:, 8:16], 1e-5, None, ALU.add)
            v.drain()
            TV.clear()
            v.sem_inc(s_v, 1)
            C["v"] += 1
        V_LN2S = C["v"]

        @block.scalar
        def _(s):
            s.wait_ge(s_v, V_LN2S)
            TS.clear()
            TS.pre(s, [elncol[:, 16:24]], [elncol[:, 8:16], cp])
            s.activation(elncol[:, 16:24], elncol[:, 8:16], ACT.Sqrt,
                         bias=cp[:, 3:4], scale=cp[:, 1:2])
            s.drain()
            TS.clear()
            s.sem_inc(s_s, 1)
            C["s"] += 1
        S_LN2 = C["s"]

        @block.vector
        def _(v):
            v.wait_ge(s_s, S_LN2)
            TV.clear()
            TV.pre(v, [elncol[:, 24:32]], [elncol[:, 16:24]])
            v.reciprocal(elncol[:, 24:32], elncol[:, 16:24])
            for p in range(NPT):
                x_t = x_res[:, p * D:(p + 1) * D]
                dst = xln_bf[:, p * D:(p + 1) * D]
                TV.pre(v, [dst],
                       [x_t, elncol[:, p:p + 1], elncol[:, 24 + p:25 + p]])
                v.tensor_scalar(dst, x_t, elncol[:, p:p + 1],
                                elncol[:, 24 + p:25 + p], ALU.subtract, ALU.mult)
            v.drain()
            TV.clear()
            v.sem_inc(s_v, 1)
            C["v"] += 1
        V_XLN = C["v"]

        etr = []
        for p in range(NPT):
            for dc in range(NDC):
                etr.append((xln_bf[:, p * D + dc * 128:p * D + dc * 128 + 128],
                            xlnT[:, dc * TC + p * 128:dc * TC + p * 128 + 128]))
        emit_transposes(etr, V_XLN, epw)

        @block.vector
        def _(v):
            v.wait_ge(s_te, NTR[0])
            TV.clear()
            v.sem_inc(s_v, 1)
            C["v"] += 1
        V_XLNT = C["v"]
        e1.close()
        e3 = ExitStack()
        h_sb = sbt(e3, "h_sb", [128, NIC * 512], BF16)

        sh_bf = sbt(e3, "sh_bf", [1, 1024], BF16)
        gu_es = ExitStack()
        g_ps = [pst(gu_es, f"g_ps{i}", [128, 512]) for i in range(2)]
        u_ps = [pst(gu_es, f"u_ps{i}", [128, 512]) for i in range(2)]
        s1_ps = [pst(gu_es, f"s1_ps{i}", [1, 512]) for i in range(2)]
        s1row = sbt(gu_es, "s1row", [1, 1024])
        s1_bf = sbt(gu_es, "s1_bf", [1, 1024], BF16)
        gwb = sbt(gu_es, "gwb", [128, 2 * D], BF16)
        uwb = sbt(gu_es, "uwb", [128, 2 * D], BF16)
        sg_sb = [sbt(gu_es, f"sg{i}", [128, 512], BF16) for i in range(2)]
        h1st = [sbt(gu_es, f"h1st{i}", [128, 512], BF16) for i in range(2)]

        @block.tensor
        def _(t):
            t.wait_ge(s_v, V_XLNT)
            for b2 in range(2):
                for dc in range(NDC):
                    mm = t.matmul(s1_ps[b2][:], ones_bf[:],
                                  xlnT[:, dc * TC + b2 * 512:dc * TC + b2 * 512 + 512],
                                  start=(dc == 0), stop=(dc == NDC - 1))
                mm.then_inc(s_f4, 1)

        @block.vector
        def _(v):
            v.wait_ge(s_f4, 2)
            TV.clear()
            for b2 in range(2):
                TV.pre(v, [s1row[:]], [s1_ps[b2]])
                v.tensor_copy(s1row[0:1, b2 * 512:(b2 + 1) * 512], s1_ps[b2][0:1, :])
            TV.pre(v, [s1_bf[:]], [s1row[:]])
            v.tensor_copy(s1_bf[0:1, :], s1row[0:1, :])
            v.drain()
            TV.clear()
            v.sem_inc(s_v, 1)
            C["v"] += 1
        V_S1 = C["v"]

        @block.gpsimd
        def _(g):
            for i in range(NIC):
                gsem = s_ga if i % 2 == 0 else s_gb
                g.wait_ge(s_v, V_XLNT)
                if i >= 2:
                    g.wait_ge(s_f1, 2 * i - 2)
                g.dma_start(out=gwb[:, (i % 2) * D:(i % 2 + 1) * D]
                            .rearrange("p (dc w) -> p dc w", dc=NDC),
                            in_=gwT_in[i].rearrange("(dc p) w -> p dc w", p=128)
                            ).then_inc(gsem, 16)
                g.dma_start(out=uwb[:, (i % 2) * D:(i % 2 + 1) * D]
                            .rearrange("p (dc w) -> p dc w", dc=NDC),
                            in_=uwT_in[i].rearrange("(dc p) w -> p dc w", p=128)
                            ).then_inc(gsem, 16)
        LDG_GU = lambda i: 32 * (i // 2 + 1)

        @block.tensor
        def _(t):
            t.wait_ge(s_v, V_S1)
            for i in range(NIC):
                t.wait_ge(s_ga if i % 2 == 0 else s_gb, LDG_GU(i))
                for b2 in range(2):
                    u = 2 * i + b2
                    if u >= 2:
                        t.wait_ge(s_f3, u - 1)
                    for dc in range(NDC):
                        t.matmul(g_ps[b2][:],
                                 gwb[:, (i % 2) * D + dc * 128:
                                     (i % 2) * D + dc * 128 + 128],
                                 xlnT[:, dc * TC + b2 * 512:dc * TC + b2 * 512 + 512],
                                 start=(dc == 0), stop=False)
                    t.matmul(g_ps[b2][:], m128_bf[0:1, :],
                             s1_bf[0:1, b2 * 512:(b2 + 1) * 512],
                             start=False, stop=True)
                    for dc in range(NDC):
                        t.matmul(u_ps[b2][:],
                                 uwb[:, (i % 2) * D + dc * 128:
                                     (i % 2) * D + dc * 128 + 128],
                                 xlnT[:, dc * TC + b2 * 512:dc * TC + b2 * 512 + 512],
                                 start=(dc == 0), stop=False)
                    mm = t.matmul(u_ps[b2][:], m128_bf[0:1, :],
                                  s1_bf[0:1, b2 * 512:(b2 + 1) * 512],
                                  start=False, stop=True)
                    mm.then_inc(s_f1, 1)
                    if u >= 2:
                        uo = u - 2
                        io, b2o = divmod(uo, 2)
                        t.wait_ge(s_f3, uo + 1)
                        src = (h_sb[:, io * 512:(io + 1) * 512] if b2o == 0
                               else h1st[io % 2][:])
                        mm = t.matmul(s1_ps[b2o][:], ones_bf[:], src,
                                      start=(io == 0), stop=(io == NIC - 1))
                        mm.then_inc(s_f4, 1)
            for uo in (2 * NIC - 2, 2 * NIC - 1):
                io, b2o = divmod(uo, 2)
                t.wait_ge(s_f3, uo + 1)
                src = (h_sb[:, io * 512:(io + 1) * 512] if b2o == 0
                       else h1st[io % 2][:])
                mm = t.matmul(s1_ps[b2o][:], ones_bf[:], src,
                              start=(io == 0), stop=(io == NIC - 1))
                mm.then_inc(s_f4, 1)

        @block.scalar
        def _(s):
            for i in range(NIC):
                for b2 in range(2):
                    u = 2 * i + b2
                    s.wait_ge(s_f1, u + 1)
                    if u >= 2:
                        s.wait_ge(s_f3, u - 1)
                    TS.clear()
                    TS.pre(s, [sg_sb[u % 2][:]], [g_ps[b2], gs_sb])
                    s.activation(sg_sb[u % 2][:], g_ps[b2][:], ACT.Silu,
                                 bias=cp[:, 3:4], scale=gs_sb[:, i:i + 1]
                                 ).then_inc(s_f2, 1)

        @block.vector
        def _(v):
            for i in range(NIC):
                for b2 in range(2):
                    u = 2 * i + b2
                    v.wait_ge(s_f2, u + 1)
                    TV.clear()
                    if b2 == 0:
                        dst = h_sb[:, i * 512:(i + 1) * 512]
                    else:
                        if i >= 2:
                            v.wait_ge(s_ha if i % 2 == 0 else s_hb,
                                      16 * (i // 2))
                            v.wait_ge(s_f4, 2 + 2 * i - 2)
                        dst = h1st[i % 2][:]
                    TV.pre(v, [dst], [u_ps[b2], us_sb, sg_sb[u % 2]])
                    v.scalar_tensor_tensor(dst, u_ps[b2][:], us_sb[:, i:i + 1],
                                           sg_sb[u % 2][:], ALU.mult, ALU.mult
                                           ).then_inc(s_f3, 1)

        @block.sync
        def _(sync):
            for i in range(NIC):
                sync.wait_ge(s_f3, 2 * i + 2)
                sync.dma_start(out=hscr[i], in_=h1st[i % 2][:]).then_inc(
                    s_ha if i % 2 == 0 else s_hb, 16)
        FH_ST = 16 * (NIC // 2)

        @block.vector
        def _(v):
            v.wait_ge(s_f4, 2 + 2 * NIC)
            TV.clear()
            for b2 in range(2):
                TV.pre(v, [s1row[:]], [s1_ps[b2]])
                v.tensor_copy(s1row[0:1, b2 * 512:(b2 + 1) * 512], s1_ps[b2][0:1, :])
            TV.pre(v, [sh_bf[:]], [s1row[:]])
            v.tensor_copy(sh_bf[0:1, :], s1row[0:1, :])
            v.drain()
            TV.clear()
            v.sem_inc(s_v, 1)
            C["v"] += 1
        V_SH = C["v"]

        gu_es.close()
        e2.close()

        dn_es = ExitStack()
        dn_ps = [pst(dn_es, f"dn{j}", [128, 512]) for j in range(4)]
        wdb = [sbt(dn_es, f"wdb{i}", [128, 8 * 512], BF16) for i in range(2)]
        h1ld = [sbt(dn_es, f"h1ld{i}", [128, 4 * 512], BF16) for i in range(2)]
        stg_bf = sbt(dn_es, "stg_bf", [128, 4 * 512], BF16)

        def dn_cnt(u):
            return 65 * (u // 64) + (u % 64) + 1

        LDG_DN0 = LDG_GU(NIC - 1)
        tr_base = NTR[0]

        NB_D = 8 * NIC // 8

        @block.gpsimd
        def _(g):
            g.wait_ge(s_ha, FH_ST)
            g.wait_ge(s_hb, FH_ST)
            for gb in range(NB_D):
                gi0 = gb * 8
                bq, i0 = divmod(gi0, NIC)
                b, qr = divmod(bq, 4)
                dsem = s_da if gb % 2 == 0 else s_db
                g.wait_ge(s_v, V_SH)
                if gb >= 2:
                    gprev = gb * 8 - 16 + 7
                    g.wait_ge(s_f6, dn_cnt(gprev))
                g.dma_start(out=wdb[gb % 2][:]
                            .rearrange("p (i w) -> p i w", i=8),
                            in_=dwT_in[i0:i0 + 8, :, qr * 512:(qr + 1) * 512]
                            .rearrange("i p w -> p i w")).then_inc(dsem, 16)
        LDG_DN = lambda gb: 16 * (gb // 2 + 1)

        @block.sync
        def _(sync):
            sync.wait_ge(s_ha, FH_ST)
            sync.wait_ge(s_hb, FH_ST)
            for lb in range(NIC):
                qr, i0 = divmod(lb * 4, NIC)
                fsem = s_fla if lb % 2 == 0 else s_flb
                sync.wait_ge(s_v, V_SH)
                if lb >= 2:
                    ug = (4 + qr) * NIC + i0 - 8 + 3
                    sync.wait_ge(s_f6, dn_cnt(max(ug, (4 + 0) * NIC - 8 + 3)))
                sync.dma_start(out=h1ld[lb % 2][:]
                               .rearrange("p (i w) -> p i w", i=4),
                               in_=hscr[i0:i0 + 4].rearrange("i p w -> p i w")
                               ).then_inc(fsem, 16)

        @block.tensor
        def _(t):
            t.wait_ge(s_v, V_SH)
            for bq in range(8):
                b, qr = divmod(bq, 4)
                for i in range(NIC):
                    gi = bq * NIC + i
                    if i % 8 == 0:
                        gb = gi // 8
                        t.wait_ge(s_da if gb % 2 == 0 else s_db, LDG_DN(gb))
                    if b == 1 and i % 4 == 0:
                        lb = (qr * NIC + i) // 4
                        t.wait_ge(s_fla if lb % 2 == 0 else s_flb,
                                  16 * (lb // 2 + 1))
                    if i == 0:
                        t.wait_ge(s_f5, bq * 4)
                    h_src = (h_sb[:, i * 512:(i + 1) * 512] if b == 0
                             else h1ld[(i // 4) % 2][:, (i % 4) * 512:
                                                      (i % 4) * 512 + 512])
                    wsl = wdb[(i // 8) % 2][:, (i % 8) * 512:(i % 8) * 512 + 512]
                    for ocl in range(4):
                        mm = t.matmul(dn_ps[ocl][:],
                                      wsl[:, ocl * 128:(ocl + 1) * 128],
                                      h_src, start=(i == 0), stop=False)
                    mm.then_inc(s_f6, 1)
                for ocl in range(4):
                    mm = t.matmul(dn_ps[ocl][:], m128_bf[0:1, :],
                                  sh_bf[0:1, b * 512:(b + 1) * 512],
                                  start=False, stop=True)
                mm.then_inc(s_f6, 1)
                t.wait_ge(s_f5, bq * 4 + 4)
                for k in range(16):
                    m = tr_base + bq * 16 + k
                    if m >= 2:
                        t.wait_ge(s_te, m - 1)
                    ocl, j4 = divmod(k, 4)
                    t.transpose(epw[m % 2][:],
                                stg_bf[:, ocl * 512 + j4 * 128:
                                       ocl * 512 + j4 * 128 + 128],
                                ident[:]).then_inc(s_tp, 1)

        @block.vector
        def _(v):
            v.wait_ge(s_ha, FH_ST)
            v.wait_ge(s_hb, FH_ST)
            for bq in range(8):
                b, qr = divmod(bq, 4)
                v.wait_ge(s_f6, 65 * bq + 65)
                if bq >= 1:
                    v.wait_ge(s_tp, tr_base + 16 * bq)
                TV.clear()
                for ocl in range(4):
                    oc_g = qr * 4 + ocl
                    dst = stg_bf[:, ocl * 512:(ocl + 1) * 512]
                    TV.pre(v, [dst], [dn_ps[ocl], ds_sb])
                    v.tensor_scalar(dst, dn_ps[ocl][:], ds_sb[:, oc_g:oc_g + 1],
                                    None, ALU.mult).then_inc(s_f5, 1)
                for k in range(16):
                    m = tr_base + bq * 16 + k
                    v.wait_ge(s_tp, m + 1)
                    TV.clear()
                    ocl, j4 = divmod(k, 4)
                    oc_g = qr * 4 + ocl
                    p_own = 4 * b + j4
                    xsl = x_res[:, p_own * D + oc_g * 128:
                                p_own * D + oc_g * 128 + 128]
                    TV.pre(v, [xsl], [xsl, epw[m % 2]])
                    v.tensor_tensor(xsl, xsl, epw[m % 2][:], ALU.add
                                    ).then_inc(s_te, 1)
            v.drain()
            TV.clear()
            v.sem_inc(s_v, 1)
            C["v"] += 1
        V_FFN = C["v"]
        NTR[0] = tr_base + 8 * 16

        @block.sync
        def _(sync):
            sync.wait_ge(s_v, V_FFN)
            dsync(sync, y_out.rearrange("(t p) d -> p t d", p=128),
                  x_res[:].rearrange("p (t d) -> p t d", t=NPT), "out")

        finish([(s_v, V_FFN)])
        dn_es.close()
        epw_es.close()
        e3.close()

    return nc, taps


_CACHE = {}


def _prog(upto="full", want_taps=False):
    key = (upto, want_taps)
    if key not in _CACHE:
        _CACHE[key] = build_program(upto, want_taps)
    return _CACHE[key]


def make_inputs(L0_attn_x, L0_attn_wq, L0_attn_wk, L0_attn_wv, L0_attn_wo,
                L0_ffn_gate_w, L0_ffn_gate_scale, L0_ffn_up_w, L0_ffn_up_scale,
                L0_ffn_down_w, L0_ffn_down_scale):
    bf = ml_dtypes.bfloat16
    x = np.ascontiguousarray(np.asarray(L0_attn_x, dtype=np.float32)[0])
    wq = np.asarray(L0_attn_wq, dtype=np.float32)
    wk = np.asarray(L0_attn_wk, dtype=np.float32)
    wv = np.asarray(L0_attn_wv, dtype=np.float32)
    woT = np.ascontiguousarray(np.asarray(L0_attn_wo, dtype=np.float32).T)
    gw = np.asarray(L0_ffn_gate_w, dtype=np.int32)
    uw = np.asarray(L0_ffn_up_w, dtype=np.int32)
    dw = np.asarray(L0_ffn_down_w, dtype=np.int32)
    gwT = np.ascontiguousarray(gw.T.reshape(D, NIC, 128).transpose(1, 0, 2))
    uwT = np.ascontiguousarray(uw.T.reshape(D, NIC, 128).transpose(1, 0, 2))
    dwT = np.ascontiguousarray(dw.T.reshape(NIC, 128, D))
    gs = np.ascontiguousarray(
        np.asarray(L0_ffn_gate_scale, np.float32).reshape(NIC, 128).T)
    us = np.ascontiguousarray(
        np.asarray(L0_ffn_up_scale, np.float32).reshape(NIC, 128).T)
    ds = np.ascontiguousarray(
        np.asarray(L0_ffn_down_scale, np.float32).reshape(NDC, 128).T)
    a_ = np.arange(0, HD, 2, dtype=np.float32)
    inv = (np.float32(1.0) / (np.float32(1e6) ** (a_ / np.float32(HD)))
           ).astype(np.float32)
    pos = np.arange(T, dtype=np.float32)
    fr = pos[:, None] * inv[None, :]
    emb = np.concatenate([fr, fr], axis=1).astype(np.float32)
    cos_t = np.cos(emb).astype(np.float32)
    sin_t = np.sin(emb).astype(np.float32)
    ident = np.eye(128, dtype=np.float32).astype(bf)
    trim = (np.arange(128)[:, None] <= np.arange(128)[None, :]).astype(bf)

    in_maps = []
    for c in range(NCORE):
        kv = c // 2
        in_maps.append({
            "x_in": np.ascontiguousarray(x[TC * c:TC * (c + 1)]),
            "wq_in": np.ascontiguousarray(wq[256 * c:256 * (c + 1)]),
            "wk_in": np.ascontiguousarray(wk[128 * kv:128 * (kv + 1)]),
            "wv_in": np.ascontiguousarray(wv[128 * kv:128 * (kv + 1)]),
            "woT_in": woT,
            "gwT_in": gwT, "uwT_in": uwT, "dwT_in": dwT,
            "gs_in": gs, "us_in": us, "ds_in": ds,
            "cos_in": cos_t, "sin_in": sin_t,
            "ident_in": ident, "tri_in": trim,
        })
    return in_maps


def kernel(**inputs):
    in_maps = make_inputs(**inputs)
    nc, _ = _prog("full", False)
    res = run_bass_kernel_spmd(nc, in_maps, core_ids=list(range(NCORE)))
    y = np.concatenate([res.results[c]["y_out"] for c in range(NCORE)], axis=0)
    return y.reshape(1, T, D)


# version 05
# speedup vs baseline: 1.4103x; 1.0455x over previous
import numpy as np
import ml_dtypes
from contextlib import ExitStack

import concourse.bass as bass
import concourse.mybir as mybir
from concourse.bass_utils import run_bass_kernel_spmd

F32 = mybir.dt.float32
BF16 = mybir.dt.bfloat16
I32 = mybir.dt.int32
U32 = mybir.dt.uint32
ALU = mybir.AluOpType
ACT = mybir.ActivationFunctionType
AX = mybir.AxisListType.X

D = 2048
T = 8192
NCORE = 8
TC = T // NCORE
HD = 128
NDC = D // 128
NPT = TC // 128
NPTG = T // 128
IF = 8192
NIC = IF // 128
SCALE = float(HD) ** -0.5
C_EXP = 12.0
NQB = T // 512


class Trk:

    def __init__(self):
        self.w, self.r = set(), set()

    @staticmethod
    def _n(aps):
        s = set()
        for ap in aps:
            t = getattr(ap, "tensor", None)
            if t is not None:
                s.add(t.name)
            elif hasattr(ap, "name") and hasattr(ap, "shape"):
                s.add(ap.name)
        return s

    def pre(self, eng, outs, ins):
        no, ni = self._n(outs), self._n(ins)
        if (ni & self.w) or (no & (self.w | self.r)):
            eng.drain()
            self.w.clear()
            self.r.clear()
        self.w |= no
        self.r |= ni

    def clear(self):
        self.w.clear()
        self.r.clear()


def build_program(upto="full", want_taps=False):
    nc = bass.Bass("TRN2", target_bir_lowering=False, debug=False,
                   enable_partition_id=True)
    es = ExitStack()
    TV, TS = Trk(), Trk()
    LVL = ["qkv", "attn", "wo", "full"].index(upto)
    C = {"ld": 0, "lds": 0, "ldg": 0, "v": 0, "s": 0, "cc": 0, "out": 0,
         "fh": 0, "lc": 0, "lw": 0, "lx": 0, "lc1": 0, "lxs": 0, "lxr": 0,
         "lt": 0}

    din = lambda n_, sh, dt: nc.dram_tensor(n_, sh, dt, kind="ExternalInput").ap()
    dint = lambda n_, sh, dt: nc.dram_tensor(n_, sh, dt, kind="Internal").ap()

    x_in = din("x_in", [TC, D], F32)
    wq_in = din("wq_in", [256, D], F32)
    wk_in = din("wk_in", [128, D], F32)
    wv_in = din("wv_in", [128, D], F32)
    woT_in = din("woT_in", [D, D], F32)
    gwT_in = din("gwT_in", [NIC, D, 128], I32)
    uwT_in = din("uwT_in", [NIC, D, 128], I32)
    dwT_in = din("dwT_in", [NIC, 128, D], I32)
    gs_in = din("gs_in", [128, NIC], F32)
    us_in = din("us_in", [128, NIC], F32)
    ds_in = din("ds_in", [128, NDC], F32)
    cos_in = din("cos_in", [T, HD], F32)
    sin_in = din("sin_in", [T, HD], F32)
    ident_in = din("ident_in", [128, 128], BF16)
    tri_in = din("tri_in", [128, 128], BF16)

    y_out = nc.dram_tensor("y_out", [TC, D], F32, kind="ExternalOutput").ap()

    cc1_in = dint("cc1_in", [D, TC], BF16)
    cc1_out = nc.dram_tensor("cc1_out", [NCORE * D, TC], BF16, kind="Internal", addr_space="Shared").ap()
    ccs_in = dint("ccs_in", [128, NPT], F32)
    ccs_out = nc.dram_tensor("ccs_out", [NCORE * 128, NPT], F32, kind="Internal", addr_space="Shared").ap()
    cc2_in = dint("cc2_in", [T, 256], BF16)
    cc2_out = nc.dram_tensor("cc2_out", [NCORE * T, 256], BF16, kind="Internal", addr_space="Shared").ap()
    hscr = dint("hscr", [NIC, 128, 512], BF16)
    wsscr = dint("wsscr", [128, 4], F32)

    taps = {}
    if want_taps:
        tapdef = {"qx": ([128, NPT * D], BF16), "xs": ([128, NPT], F32),
                  "qkv": ([128, 3 * T + 64 * 129], BF16),
                  "attn": ([128, 64 * 256], BF16),
                  "wo": ([128, NPT * D], F32)}
        for name, (sh, dt) in tapdef.items():
            taps[name] = nc.dram_tensor("tap_" + name, sh, dt,
                                        kind="ExternalOutput").ap()

    sbt = lambda st, n_, sh, dt=F32: st.enter_context(nc.sbuf_tensor(n_, sh, dt))
    sbr = lambda st, n_, sh, dt=F32: st.enter_context(
        nc.sbuf_tensor(n_, sh, dt, side="right"))
    pst = lambda st, n_, sh, dt=F32: st.enter_context(nc.psum_tensor(n_, sh, dt))

    ident = sbt(es, "ident", [128, 128], BF16)
    tri = sbt(es, "tri", [128, 128], BF16)
    cp = sbt(es, "cp", [128, 4])
    ones_bf = sbt(es, "ones_bf", [128, 1], BF16)
    onesf_row = sbt(es, "onesf_row", [1, 128], F32)
    m128_bf = sbt(es, "m128_bf", [1, 128], BF16)
    pid_sb = sbt(es, "pid_sb", [1, 1], U32)
    xs_all = sbt(es, "xs_all", [128, NPTG])
    gs_sb = sbt(es, "gs_sb", [128, NIC])
    us_sb = sbt(es, "us_sb", [128, NIC])
    ds_sb = sbt(es, "ds_sb", [128, NDC])

    sem = lambda n_: es.enter_context(nc.semaphore(n_))
    s_ld = sem("s_ld")
    s_lc = sem("s_lc")
    s_lw = sem("s_lw")
    s_lx = sem("s_lx")
    s_lc1 = sem("s_lc1")
    s_lxs = sem("s_lxs")
    s_lxr = sem("s_lxr")
    s_lt = sem("s_lt")
    s_lda = sem("s_lda")
    s_ldb = sem("s_ldb")
    s_ga = sem("s_ga")
    s_gb = sem("s_gb")
    s_da = sem("s_da")
    s_db = sem("s_db")
    s_fla = sem("s_fla")
    s_flb = sem("s_flb")
    s_ha = sem("s_ha")
    s_hb = sem("s_hb")
    s_wsa = sem("s_wsa")
    s_vb = sem("s_vb")
    s_sb2 = sem("s_sb2")
    s_wsb = sem("s_wsb")
    s_tw = sem("s_tw")
    s_lds = sem("s_lds")
    s_ldg = sem("s_ldg")
    s_v = sem("s_v")
    s_s = sem("s_s")
    s_cc = sem("s_cc")
    s_mm = sem("s_mm")
    s_rp = sem("s_rp")
    s_tp = sem("s_tp")
    s_te = sem("s_te")
    s_qb = sem("s_qb")
    s_qk = sem("s_qk")
    s_exp = sem("s_exp")
    s_msk = sem("s_msk")
    s_pv = sem("s_pv")
    s_pvd = sem("s_pvd")
    s_wo = sem("s_wo")
    s_wov = sem("s_wov")
    s_f1 = sem("s_f1")
    s_f2 = sem("s_f2")
    s_f3 = sem("s_f3")
    s_f4 = sem("s_f4")
    s_f5 = sem("s_f5")
    s_f6 = sem("s_f6")
    s_fh = sem("s_fh")
    s_out = sem("s_out")

    def dsync(eng, out, in_, cnt="ld", k=16):
        eng.dma_start(out=out, in_=in_).then_inc(
            {"ld": s_ld, "lds": s_lds, "ldg": s_ldg, "out": s_out,
             "fh": s_fh, "lc": s_lc, "lw": s_lw, "lx": s_lx, "lc1": s_lc1,
             "lxs": s_lxs, "lxr": s_lxr, "lt": s_lt}[cnt], k)
        C[cnt] += k
        return C[cnt]

    with es, nc.Block() as block:
        @block.sync
        def _(sync):
            for dst, src in ((ident, ident_in), (tri, tri_in), (gs_sb, gs_in),
                             (us_sb, us_in), (ds_sb, ds_in)):
                dsync(sync, dst[:], src[:], "lc")
            dsync(sync, pid_sb[:], nc.partition_id_tensor.ap(), "lc")
        LD_CONST = C["lc"]

        @block.vector
        def _(v):
            for j, val in enumerate((1e-5, 1.0, -C_EXP, 0.0)):
                TV.pre(v, [cp[:, j:j + 1]], [])
                v.memset(cp[:, j:j + 1], val)
            TV.pre(v, [ones_bf[:]], [])
            v.memset(ones_bf[:], 1.0)
            TV.pre(v, [onesf_row[:]], [])
            v.memset(onesf_row[:], 1.0)
            TV.pre(v, [m128_bf[:]], [])
            v.memset(m128_bf[:], -128.0)
            v.drain()
            TV.clear()
            v.sem_inc(s_v, 1)
            C["v"] += 1
        V_CONST = C["v"]

        wt_es = ExitStack()
        wT_all = wt_es.enter_context(nc.sbuf_tensor("wT_all", [128, NDC * 512], BF16))
        wsrep_es = ExitStack()
        ws_rep = wsrep_es.enter_context(nc.sbuf_tensor("ws_rep", [128, 512], F32))
        pw_es = ExitStack()
        pw = [pst(pw_es, f"pw{i}", [128, 128], BF16) for i in range(4)]
        NTR = [0]

        w_es = ExitStack()
        w_f = sbt(w_es, "w_f", [128, 4 * D])
        wdq_bf = sbt(w_es, "wdq_bf", [128, 4 * D], BF16)
        wscr = sbt(w_es, "wscr", [128, D])
        wscri = sbt(w_es, "wscri", [128, D], I32)
        colsc = sbt(w_es, "colsc", [128, 16])

        @block.sync
        def _(sync):
            dsync(sync, w_f[:, 0:2 * D].rearrange("p (t d) -> p t d", t=2),
                  wq_in.rearrange("(t p) d -> p t d", p=128), "lw")
            dsync(sync, w_f[:, 2 * D:3 * D], wk_in[:], "lw")
            dsync(sync, w_f[:, 3 * D:4 * D], wv_in[:], "lw")
        LD_W = C["lw"]

        @block.vector
        def _(v):
            v.wait_ge(s_lw, LD_W)
            TV.clear()
            for i in range(4):
                w_t = w_f[:, i * D:(i + 1) * D]
                TV.pre(v, [colsc[:, i:i + 1]], [w_t])
                v.reduce_max(colsc[:, i:i + 1].rearrange("p (a o) -> p a o", a=1),
                             w_t.rearrange("p (a d) -> p a d", a=1), axis=AX,
                             apply_absolute_value=True)
            TV.pre(v, [colsc[:, 4:8]], [colsc[:, 0:4]])
            v.tensor_scalar(colsc[:, 4:8], colsc[:, 0:4], 1.0 / 127.0, 1e-8,
                            ALU.mult, ALU.max)
            TV.pre(v, [colsc[:, 8:12]], [colsc[:, 4:8]])
            v.reciprocal(colsc[:, 8:12], colsc[:, 4:8])
            for i in range(4):
                w_t = w_f[:, i * D:(i + 1) * D]
                TV.pre(v, [wscr[:]], [w_t, colsc[:, 8 + i:9 + i]])
                v.tensor_scalar(wscr[:], w_t, colsc[:, 8 + i:9 + i], None, ALU.mult)
                TV.pre(v, [wscr[:]], [wscr[:]])
                v.tensor_scalar(wscr[:], wscr[:], 12582912.0, 12582912.0,
                                ALU.add, ALU.subtract)
                TV.pre(v, [wscri[:]], [wscr[:]])
                v.tensor_copy(wscri[:], wscr[:])
                TV.pre(v, [wdq_bf[:, i * D:(i + 1) * D]], [wscri[:]])
                v.tensor_copy(wdq_bf[:, i * D:(i + 1) * D], wscri[:])
            v.drain()
            TV.clear()
            v.sem_inc(s_v, 1)
            C["v"] += 1
        V_WQ = C["v"]

        ws_row = w_es.enter_context(nc.sbuf_tensor("ws_row", [1, 512], F32))
        pwf = w_es.enter_context(nc.psum_tensor("pwf", [128, 512], F32))

        @block.sync
        def _(sync):
            sync.wait_ge(s_v, V_WQ)
            sync.dma_start(out=wsscr[:], in_=colsc[:, 4:8]).then_inc(s_wsa, 16)
            sync.wait_ge(s_wsa, 16)
            sync.dma_start(
                out=ws_row[0:1, :].rearrange("a (i p) -> a i p", i=4),
                in_=wsscr.rearrange("(a p) i -> a i p", a=1),
                allow_slow_non_contiguous=True).then_inc(s_wsb, 16)

        @block.tensor
        def _(t):
            t.wait_ge(s_wsb, 16)
            t.wait_ge(s_v, V_CONST)
            t.matmul(pwf[:], onesf_row[0:1, :], ws_row[0:1, :],
                     start=True, stop=True).then_inc(s_tw, 1)

        @block.vector
        def _(v):
            v.wait_ge(s_tw, 1)
            TV.clear()
            TV.pre(v, [ws_rep[:]], [pwf])
            v.tensor_copy(ws_rep[:], pwf[:])
            v.drain()
            TV.clear()
            v.sem_inc(s_v, 1)
            C["v"] += 1
        V_WSREP = C["v"]

        def emit_transposes(items, wait_v, psums):
            base = NTR[0]
            n_items = len(items)

            nb = len(psums)

            @block.tensor
            def _(t):
                t.wait_ge(s_v, wait_v)
                t.wait_ge(s_lc, LD_CONST)
                for k in range(n_items):
                    m = base + k
                    if m >= nb:
                        t.wait_ge(s_te, m - nb + 1)
                    t.transpose(psums[m % nb][:], items[k][0],
                                ident[:]).then_inc(s_tp, 1)

            @block.vector
            def _(v):
                for k in range(n_items):
                    m = base + k
                    v.wait_ge(s_tp, m + 1)
                    TV.clear()
                    TV.pre(v, [items[k][1]], [])
                    v.tensor_copy(items[k][1], psums[m % nb][:]).then_inc(s_te, 1)

            NTR[0] += n_items

        wtr = []
        for i in range(4):
            for dc in range(NDC):
                wtr.append((wdq_bf[:, i * D + dc * 128:i * D + dc * 128 + 128],
                            wT_all[:, dc * 512 + i * 128:dc * 512 + i * 128 + 128]))
        emit_transposes(wtr, V_WQ, pw)

        @block.vector
        def _(v):
            v.wait_ge(s_te, NTR[0])
            TV.clear()
            v.sem_inc(s_v, 1)
            C["v"] += 1
        V_WT = C["v"]
        w_es.close()

        a_es = ExitStack()
        xa = sbt(a_es, "xa", [128, NPT * D])
        qx_sb = sbt(a_es, "qx_sb", [128, NPT * D], BF16)
        qxT_own = sbt(a_es, "qxT_own", [128, NDC * TC], BF16)
        scr_a = sbt(a_es, "scr_a", [128, D])
        scr_i = sbt(a_es, "scr_i", [128, D], I32)
        lncol = sbt(a_es, "lncol", [128, 40])
        xs_own = sbt(a_es, "xs_own", [128, NPT])

        @block.sync
        def _(sync):
            sync.wait_ge(s_v, V_WT)
            dsync(sync, xa[:].rearrange("p (t d) -> p t d", t=NPT),
                  x_in.rearrange("(t p) d -> p t d", p=128), "lx")
        LD_X = C["lx"]

        @block.vector
        def _(v):
            v.wait_ge(s_lx, LD_X)
            TV.clear()
            for p in range(NPT):
                x_t = xa[:, p * D:(p + 1) * D]
                TV.pre(v, [scr_a[:], lncol[:, 8 + p:9 + p]], [x_t])
                v.scalar_tensor_tensor(scr_a[:], x_t, 1.0, x_t, ALU.bypass,
                                       ALU.mult, accum_out=lncol[:, 8 + p:9 + p])
                TV.pre(v, [lncol[:, p:p + 1]], [x_t])
                v.reduce_sum(lncol[:, p:p + 1].rearrange("p (a o) -> p a o", a=1),
                             x_t.rearrange("p (a d) -> p a d", a=1), axis=AX)
                TV.pre(v, [lncol[:, 16 + p:17 + p]], [x_t])
                v.reduce_max(lncol[:, 16 + p:17 + p].rearrange("p (a o) -> p a o", a=1),
                             x_t.rearrange("p (a d) -> p a d", a=1), axis=AX)
                TV.pre(v, [lncol[:, 24 + p:25 + p]], [x_t])
                v.tensor_reduce(lncol[:, 24 + p:25 + p].rearrange("p (a o) -> p a o", a=1),
                                x_t.rearrange("p (a d) -> p a d", a=1), AX, ALU.min)
            TV.pre(v, [lncol[:, 0:8]], [lncol[:, 0:8]])
            v.tensor_scalar(lncol[:, 0:8], lncol[:, 0:8], 1.0 / D, None, ALU.mult)
            TV.pre(v, [lncol[:, 32:40]], [lncol[:, 0:8]])
            v.tensor_tensor(lncol[:, 32:40], lncol[:, 0:8], lncol[:, 0:8], ALU.mult)
            TV.pre(v, [lncol[:, 8:16]], [lncol[:, 8:16], lncol[:, 32:40]])
            v.scalar_tensor_tensor(lncol[:, 8:16], lncol[:, 8:16], 1.0 / D,
                                   lncol[:, 32:40], ALU.mult, ALU.subtract)
            TV.pre(v, [lncol[:, 8:16]], [lncol[:, 8:16]])
            v.tensor_scalar(lncol[:, 8:16], lncol[:, 8:16], 1e-5, None, ALU.add)
            TV.pre(v, [lncol[:, 16:24]], [lncol[:, 16:24], lncol[:, 0:8]])
            v.tensor_tensor(lncol[:, 16:24], lncol[:, 16:24], lncol[:, 0:8],
                            ALU.subtract)
            TV.pre(v, [lncol[:, 24:32]], [lncol[:, 0:8], lncol[:, 24:32]])
            v.tensor_tensor(lncol[:, 24:32], lncol[:, 0:8], lncol[:, 24:32],
                            ALU.subtract)
            TV.pre(v, [lncol[:, 16:24]], [lncol[:, 16:24], lncol[:, 24:32]])
            v.tensor_tensor(lncol[:, 16:24], lncol[:, 16:24], lncol[:, 24:32],
                            ALU.max)
            v.drain()
            TV.clear()
            v.sem_inc(s_v, 1)
            C["v"] += 1
        V_LN1 = C["v"]

        @block.scalar
        def _(s):
            s.wait_ge(s_v, V_CONST)
            s.wait_ge(s_v, V_LN1)
            TS.clear()
            TS.pre(s, [lncol[:, 32:40]], [lncol[:, 8:16], cp])
            s.activation(lncol[:, 32:40], lncol[:, 8:16], ACT.Sqrt,
                         bias=cp[:, 3:4], scale=cp[:, 1:2])
            s.drain()
            TS.clear()
            s.sem_inc(s_s, 1)
            C["s"] += 1
        S_LN1 = C["s"]

        @block.vector
        def _(v):
            v.wait_ge(s_s, S_LN1)
            TV.clear()
            TV.pre(v, [lncol[:, 24:32]], [lncol[:, 32:40]])
            v.reciprocal(lncol[:, 24:32], lncol[:, 32:40])
            TV.pre(v, [xs_own[:]], [lncol[:, 16:24], lncol[:, 24:32]])
            v.tensor_tensor(xs_own[:], lncol[:, 16:24], lncol[:, 24:32], ALU.mult)
            TV.pre(v, [xs_own[:]], [xs_own[:]])
            v.tensor_scalar(xs_own[:], xs_own[:], 1.0 / 127.0, 1e-8,
                            ALU.mult, ALU.max)
            TV.pre(v, [lncol[:, 32:40]], [lncol[:, 16:24]])
            v.reciprocal(lncol[:, 32:40], lncol[:, 16:24])
            TV.pre(v, [lncol[:, 32:40]], [lncol[:, 32:40]])
            v.tensor_scalar(lncol[:, 32:40], lncol[:, 32:40], 127.0, None, ALU.mult)
            TV.pre(v, [lncol[:, 16:24]], [lncol[:, 0:8], lncol[:, 32:40]])
            v.tensor_tensor(lncol[:, 16:24], lncol[:, 0:8], lncol[:, 32:40],
                            ALU.mult)
            for p in range(NPT):
                x_t = xa[:, p * D:(p + 1) * D]
                TV.pre(v, [scr_a[:]],
                       [x_t, lncol[:, 32 + p:33 + p], lncol[:, 16 + p:17 + p]])
                v.tensor_scalar(scr_a[:], x_t, lncol[:, 32 + p:33 + p],
                                lncol[:, 16 + p:17 + p], ALU.mult, ALU.subtract)
                TV.pre(v, [scr_a[:]], [scr_a[:]])
                v.tensor_scalar(scr_a[:], scr_a[:], 12582912.0, 12582912.0,
                                ALU.add, ALU.subtract)
                TV.pre(v, [scr_i[:]], [scr_a[:]])
                v.tensor_copy(scr_i[:], scr_a[:])
                TV.pre(v, [qx_sb[:, p * D:(p + 1) * D]], [scr_i[:]])
                v.tensor_copy(qx_sb[:, p * D:(p + 1) * D], scr_i[:])
            v.drain()
            TV.clear()
            v.sem_inc(s_v, 1)
            C["v"] += 1
        V_QX = C["v"]

        qtr = []
        for p in range(NPT):
            for dc in range(NDC):
                qtr.append((qx_sb[:, p * D + dc * 128:p * D + dc * 128 + 128],
                            qxT_own[:, dc * TC + p * 128:dc * TC + p * 128 + 128]))
        emit_transposes(qtr, V_QX, pw)

        @block.vector
        def _(v):
            v.wait_ge(s_te, NTR[0])
            TV.clear()
            v.sem_inc(s_v, 1)
            C["v"] += 1
        V_QXT = C["v"]

        @block.sync
        def _(sync):
            sync.wait_ge(s_v, V_QXT)
            dsync(sync, cc1_in.rearrange("(dc p) t -> p dc t", p=128),
                  qxT_own[:].rearrange("p (dc t) -> p dc t", dc=NDC), "lc1")
            dsync(sync, ccs_in[:], xs_own[:], "lc1")
            if want_taps:
                dsync(sync, taps["qx"][:], qx_sb[:], "lc1")
                dsync(sync, taps["xs"][:], xs_own[:], "lc1")
        LD_CC1 = C["lc1"]

        @block.gpsimd
        def _(g):
            g.wait_ge(s_lc1, LD_CC1)
            g.collective_compute(
                "AllGather", ALU.bypass, replica_groups=[list(range(NCORE))],
                ins=[cc1_in[:]], outs=[cc1_out[:]]).then_inc(s_cc, 1)
            g.collective_compute(
                "AllGather", ALU.bypass, replica_groups=[list(range(NCORE))],
                ins=[ccs_in[:]], outs=[ccs_out[:]]).then_inc(s_cc, 1)
            C["cc"] += 2
        CC1 = C["cc"]

        @block.sync
        def _(sync):
            sync.wait_ge(s_cc, CC1)
            dsync(sync, xs_all[:].rearrange("p (r q) -> p r q", r=NCORE),
                  ccs_out.rearrange("(r p) q -> p r q", p=128), "lxs")
        LD_XS = C["lxs"]
        a_es.close()

        qkv_es = ExitStack()
        qT_sb = sbr(qkv_es, "qT_sb", [128, 2 * T], BF16)
        kT_sb = sbr(qkv_es, "kT_sb", [128, T], BF16)
        v_sb = sbr(qkv_es, "v_sb", [128, 64 * 129], BF16)

        b_es = ExitStack()
        rk_buf = sbt(b_es, "rk_buf", [128, 2 * NDC * TC], BF16)
        cs_buf = sbt(b_es, "cs_buf", [128, 2 * 2048])
        rp_f = sbt(b_es, "rp_f", [128, 512])
        rp_dq = sbt(b_es, "rp_dq", [128, 512])
        rp_t1 = sbt(b_es, "rp_t1", [128, 192])
        rp_t2 = sbt(b_es, "rp_t2", [128, 192])
        qv_i = sbt(b_es, "qv_i", [128, 1024], I32)
        qv_a = sbt(b_es, "qv_a", [128, 1024])
        qkv_bf = sbt(b_es, "qkv_bf", [128, 2 * 512], BF16)
        qcols = sbt(b_es, "qcols", [128, 32])
        pqkv = [pst(b_es, f"pqkv{i}", [128, 512]) for i in range(2)]

        @block.vector
        def _(v):
            v.wait_ge(s_cc, CC1)
            TV.clear()
            vo = v_sb[:].rearrange("p (c w) -> p c w", w=129)[:, :, 128:129]
            TV.pre(v, [vo], [])
            v.memset(vo, 1.0)
            v.drain()
            TV.clear()
            v.sem_inc(s_v, 1)
            C["v"] += 1
        V_VONES = C["v"]

        LD_B0 = C["ld"]

        @block.sync
        def _(sync):
            for r in range(NCORE):
                half = r % 2
                sb_sem = s_lda if half == 0 else s_ldb
                sync.wait_ge(s_cc, CC1)
                if r >= 2:
                    sync.wait_ge(s_qb, (r - 1) * NPT)
                sync.dma_start(
                    out=rk_buf[:, half * NDC * TC:(half + 1) * NDC * TC]
                    .rearrange("p (dc t) -> p dc t", dc=NDC),
                    in_=cc1_out[r * D:(r + 1) * D, :]
                    .rearrange("(dc p) t -> p dc t", p=128)).then_inc(sb_sem, 16)
                sync.dma_start(
                    out=cs_buf[:, half * 2048:half * 2048 + 1024]
                    .rearrange("p (q d) -> p q d", q=NPT),
                    in_=cos_in[r * TC:(r + 1) * TC, :]
                    .rearrange("(q p) d -> p q d", p=128)).then_inc(sb_sem, 16)
                sync.dma_start(
                    out=cs_buf[:, half * 2048 + 1024:half * 2048 + 2048]
                    .rearrange("p (q d) -> p q d", q=NPT),
                    in_=sin_in[r * TC:(r + 1) * TC, :]
                    .rearrange("(q p) d -> p q d", p=128)).then_inc(sb_sem, 16)
        LD_RK = lambda r: 48 * (r // 2 + 1)

        base_b = NTR[0]

        @block.tensor
        def _(t):
            t.wait_ge(s_lxs, LD_XS)
            t.wait_ge(s_v, V_QXT)
            for g in range(NPTG + 1):
                if g < NPTG:
                    r, pt = divmod(g, NPT)
                    half, bank = r % 2, g % 2
                    if pt == 0:
                        t.wait_ge(s_lda if r % 2 == 0 else s_ldb, LD_RK(r))
                    if g >= 2:
                        t.wait_ge(s_rp, g - 1)
                    for dc in range(NDC):
                        mm = t.matmul(
                            pqkv[bank][:],
                            rk_buf[:, half * NDC * TC + dc * TC + pt * 128:
                                   half * NDC * TC + dc * TC + pt * 128 + 128],
                            wT_all[:, dc * 512:(dc + 1) * 512],
                            start=(dc == 0), stop=(dc == NDC - 1))
                    mm.then_inc(s_mm, 1)
                if g >= 1:
                    tq = g - 1
                    t.wait_ge(s_qb, tq + 1)
                    for hh in range(3):
                        m = base_b + tq * 3 + hh
                        if m >= 4:
                            t.wait_ge(s_te, m - 3)
                        t.transpose(pw[m % 4][:],
                                    qkv_bf[:, (tq % 2) * 512 + hh * 128:
                                           (tq % 2) * 512 + hh * 128 + 128],
                                    ident[:]).then_inc(s_tp, 1)

        @block.vector
        def _(v):
            v.wait_ge(s_v, V_VONES)
            for g in range(NPTG + 1):
                if g < NPTG:
                    r, pt = divmod(g, NPT)
                    half, bank = r % 2, g % 2
                    qc = qcols[:, 16 * bank:16 * bank + 16]
                    qa = qv_a[:, 512 * bank:512 * bank + 512]
                    v.wait_ge(s_mm, g + 1)
                    v.wait_ge(s_sb2, g)
                    TV.clear()
                    TV.pre(v, [rp_dq[:]], [pqkv[bank], ws_rep])
                    v.tensor_tensor(rp_dq[:], pqkv[bank][:], ws_rep[:], ALU.mult)
                    v.drain()
                    TV.clear()
                    v.sem_inc(s_rp, 1)
                    W = rp_dq[:]
                    W3 = W.rearrange("p (h d) -> p h d", h=4)
                    cosb = cs_buf[:, half * 2048 + pt * 128:
                                  half * 2048 + pt * 128 + 128]
                    sinb = cs_buf[:, half * 2048 + 1024 + pt * 128:
                                  half * 2048 + 1024 + pt * 128 + 128]
                    bc = lambda a, w: a.rearrange("p (o d) -> p o d", o=1)\
                        .broadcast_to((128, 3, w))
                    t13 = rp_t1[:].rearrange("p (h d) -> p h d", h=3)
                    t23 = rp_t2[:].rearrange("p (h d) -> p h d", h=3)
                    rp3 = rp_f[:].rearrange("p (h d) -> p h d", h=4)
                    TV.pre(v, [rp_t1[:]], [])
                    v.tensor_tensor(t13, W3[:, 0:3, 0:64], bc(cosb[:, 0:64], 64),
                                    ALU.mult)
                    TV.pre(v, [rp_t2[:]], [])
                    v.tensor_tensor(t23, W3[:, 0:3, 64:128], bc(sinb[:, 0:64], 64),
                                    ALU.mult)
                    TV.pre(v, [rp_f[:]], [rp_t1[:], rp_t2[:]])
                    v.tensor_tensor(rp3[:, 0:3, 0:64], t13, t23, ALU.subtract)
                    TV.pre(v, [rp_t1[:]], [rp_f[:]])
                    v.tensor_tensor(t13, W3[:, 0:3, 64:128], bc(cosb[:, 64:128], 64),
                                    ALU.mult)
                    TV.pre(v, [rp_t2[:]], [rp_t1[:]])
                    v.tensor_tensor(t23, W3[:, 0:3, 0:64], bc(sinb[:, 64:128], 64),
                                    ALU.mult)
                    TV.pre(v, [rp_f[:]], [rp_t1[:], rp_t2[:]])
                    v.tensor_tensor(rp3[:, 0:3, 64:128], t13, t23, ALU.add)
                    v.wait_ge(s_sb2, g + 1)
                    TV.pre(v, [qc[:, 0:4]], [rp_f[:]])
                    v.reduce_max(qc[:, 0:4].rearrange("p (h o) -> p h o", o=1),
                                 rp_f[:].rearrange("p (h d) -> p h d", h=4),
                                 axis=AX, apply_absolute_value=True)
                    xsb = xs_all[:, g:g + 1].rearrange("p (a o) -> p a o", a=1)\
                        .broadcast_to((128, 4, 1))
                    TV.pre(v, [qc[:, 4:8]], [qc[:, 0:4], xs_all])
                    v.tensor_tensor(qc[:, 4:8].rearrange("p (h o) -> p h o", o=1),
                                    qc[:, 0:4].rearrange("p (h o) -> p h o", o=1),
                                    xsb, ALU.mult)
                    TV.pre(v, [qc[:, 4:8]], [qc[:, 4:8]])
                    v.tensor_scalar(qc[:, 4:8], qc[:, 4:8], 1.0 / 127.0,
                                    1e-8, ALU.mult, ALU.max)
                    TV.pre(v, [qc[:, 8:10]], [qc[:, 4:6]])
                    v.tensor_scalar(qc[:, 8:10], qc[:, 4:6], SCALE, None,
                                    ALU.mult)
                    TV.pre(v, [qc[:, 10:12]], [qc[:, 6:8]])
                    v.tensor_copy(qc[:, 10:12], qc[:, 6:8])
                    TV.pre(v, [qc[:, 12:16]], [qc[:, 0:4]])
                    v.reciprocal(qc[:, 12:16], qc[:, 0:4])
                    TV.pre(v, [qc[:, 12:16]], [qc[:, 12:16]])
                    v.tensor_scalar(qc[:, 12:16], qc[:, 12:16], 127.0, None,
                                    ALU.mult)
                    TV.pre(v, [qa], [rp_f[:], qc[:, 12:16]])
                    v.tensor_tensor(
                        qa.rearrange("p (h d) -> p h d", h=4),
                        rp_f[:].rearrange("p (h d) -> p h d", h=4),
                        qc[:, 12:16].rearrange("p (h o) -> p h o", o=1)
                        .broadcast_to((128, 4, 128)), ALU.mult)
                    TV.pre(v, [qa], [qa])
                    v.tensor_scalar(qa, qa, 12582912.0, 12582912.0,
                                    ALU.add, ALU.subtract)
                    v.drain()
                    TV.clear()
                    v.sem_inc(s_vb, 1)
                if g >= 1:
                    tq = g - 1
                    for hh in range(3):
                        m = base_b + tq * 3 + hh
                        v.wait_ge(s_tp, m + 1)
                        TV.clear()
                        if hh < 2:
                            dst = qT_sb[:, hh * T + tq * 128:hh * T + tq * 128 + 128]
                        else:
                            dst = kT_sb[:, tq * 128:tq * 128 + 128]
                        TV.pre(v, [dst], [])
                        v.tensor_copy(dst, pw[m % 4][:]).then_inc(s_te, 1)
            v.drain()
            TV.clear()
            v.sem_inc(s_v, 1)
            C["v"] += 1
        V_QKV = C["v"]

        @block.scalar
        def _(s):
            s.wait_ge(s_v, V_CONST)
            for g in range(NPTG):
                bank = g % 2
                qc = qcols[:, 16 * bank:16 * bank + 16]
                qa = qv_a[:, 512 * bank:512 * bank + 512]
                qi = qv_i[:, 512 * bank:512 * bank + 512]
                s.wait_ge(s_rp, g + 1)
                s.wait_ge(s_vb, g)
                TS.clear()
                TS.pre(s, [rp_f[:]], [rp_dq[:], cp])
                s.activation(rp_f[:, 384:512], rp_dq[:, 384:512], ACT.Copy,
                             bias=0.0, scale=1.0)
                s.drain()
                TS.clear()
                s.sem_inc(s_sb2, 1)
                s.wait_ge(s_vb, g + 1)
                TS.pre(s, [qi], [qa, cp])
                s.activation(qi, qa, ACT.Copy, bias=0.0, scale=1.0)
                if g >= 2:
                    s.wait_ge(s_tp, base_b + 3 * (g - 1))
                s.drain()
                TS.clear()
                for h4 in range(4):
                    dst = qkv_bf[:, bank * 512 + h4 * 128:
                                 bank * 512 + (h4 + 1) * 128]
                    TS.pre(s, [dst], [qv_i, qc, cp])
                    s.activation(dst, qi[:, h4 * 128:(h4 + 1) * 128], ACT.Copy,
                                 bias=0.0, scale=qc[:, 8 + h4:9 + h4])
                dstv = v_sb[:, g * 129:g * 129 + 128]
                TS.pre(s, [dstv], [qkv_bf, cp])
                s.activation(dstv, qkv_bf[:, bank * 512 + 384:bank * 512 + 512],
                             ACT.Copy, bias=0.0, scale=1.0)
                s.drain()
                TS.clear()
                s.sem_inc(s_qb, 1)
        NTR[0] = base_b + NPTG * 3

        if want_taps:
            @block.sync
            def _(sync):
                sync.wait_ge(s_v, V_QKV)
                tp = taps["qkv"]
                dsync(sync, tp[:, 0:2 * T], qT_sb[:], "lt")
                dsync(sync, tp[:, 2 * T:3 * T], kT_sb[:], "lt")
                dsync(sync, tp[:, 3 * T:3 * T + 64 * 129], v_sb[:], "lt")

        def finish(extra=()):
            tot = [(s_ld, C["ld"]), (s_lds, C["lds"]), (s_ldg, C["ldg"]),
                   (s_out, C["out"]), (s_lt, C["lt"]), (s_lc1, C["lc1"]),
                   (s_fh, C["fh"])]

            @block.gpsimd
            def _(g):
                for semh, val in extra:
                    g.wait_ge(semh, val)
                for semh, val in tot:
                    if val:
                        g.wait_ge(semh, val)

        if LVL == 0:
            finish([(s_v, V_QKV)])
            b_es.close()
            pw_es.close()
            wsrep_es.close()
            wt_es.close()
            qkv_es.close()
            return nc, taps

        b_es.close()
        pw_es.close()
        wsrep_es.close()
        wt_es.close()

        c_es = ExitStack()
        pT = [sbr(c_es, f"pT{i}", [128, 1024], BF16) for i in range(2)]
        stp = [pst(c_es, f"st{i}", [128, 1024]) for i in range(2)]
        pv_ps = [pst(c_es, f"pv{j}", [128, 512]) for j in range(4)]
        rcol = sbr(c_es, "rcol", [128, 4])
        attn_sb = c_es.enter_context(
            nc.sbuf_tensor("attn_sb", [128, 64 * 256], BF16, side="right"))

        halves = []
        for qb in range(NQB):
            for h in range(2):
                for kb in range(qb + 1):
                    for hf in range(2):
                        halves.append((qb, h, kb, hf, kb == qb))
        NHALF = len(halves)
        grp_end = {}
        nmsk = [0] * (NHALF + 1)
        ngrp = [0] * (NHALF + 1)
        for n, (qb, h, kb, hf, diag) in enumerate(halves):
            nmsk[n + 1] = nmsk[n] + (2 if diag else 0)
            ngrp[n + 1] = ngrp[n]
            if diag and hf == 1:
                grp_end[n] = (qb, h)
                ngrp[n + 1] += 1

        @block.tensor
        def _(t):
            t.wait_ge(s_v, V_QKV)
            t.wait_ge(s_qb, NPTG)

            def emit_qk(n):
                qb, h, kb, hf, diag = halves[n]
                if n >= 2:
                    t.wait_ge(s_exp, n - 1)
                for ci in range(2):
                    i = 2 * hf + ci
                    mm = t.matmul(
                        stp[hf][:, ci * 512:(ci + 1) * 512],
                        kT_sb[:, kb * 512 + i * 128:kb * 512 + i * 128 + 128],
                        qT_sb[:, h * T + qb * 512:h * T + qb * 512 + 512],
                        start=True, stop=True)
                mm.then_inc(s_qk, 1)

            def emit_pv(n):
                qb, h, kb, hf, diag = halves[n]
                t.wait_ge(s_exp, n + 1)
                if diag:
                    t.wait_ge(s_msk, nmsk[n + 1])
                if kb == 0 and hf == 0:
                    t.wait_ge(s_pvd, ngrp[n])
                last = None
                for ci in range(2):
                    i = 2 * hf + ci
                    for j in range(4):
                        if diag and j < i:
                            continue
                        last = t.matmul(
                            pv_ps[j][:, 0:129],
                            pT[hf][:, ci * 512 + j * 128:ci * 512 + j * 128 + 128],
                            v_sb[:, (kb * 4 + i) * 129:(kb * 4 + i) * 129 + 129],
                            start=(kb == 0 and i == 0),
                            stop=(diag and i == j))
                last.then_inc(s_pv, 1)

            emit_qk(0)
            for n in range(NHALF):
                if n + 1 < NHALF:
                    emit_qk(n + 1)
                emit_pv(n)

        @block.scalar
        def _(s):
            s.wait_ge(s_v, V_CONST)
            for n in range(NHALF):
                s.wait_ge(s_qk, n + 1)
                if n >= 2:
                    s.wait_ge(s_pv, n - 1)
                TS.clear()
                TS.pre(s, [pT[n % 2][:]], [stp[n % 2][:], cp])
                s.activation(pT[n % 2][:], stp[n % 2][:], ACT.Exp,
                             bias=cp[:, 2:3], scale=cp[:, 1:2]).then_inc(s_exp, 1)

        @block.vector
        def _(v):
            for n, (qb, h, kb, hf, diag) in enumerate(halves):
                if diag:
                    v.wait_ge(s_exp, n + 1)
                    TV.clear()
                    for ci in range(2):
                        i = 2 * hf + ci
                        blk = pT[hf][:, ci * 512 + i * 128:ci * 512 + i * 128 + 128]
                        TV.pre(v, [blk], [blk, tri])
                        v.tensor_tensor(blk, blk, tri[:], ALU.mult).then_inc(s_msk, 1)
                if n in grp_end:
                    v.wait_ge(s_pv, n + 1)
                    TV.clear()
                    for j in range(4):
                        TV.pre(v, [rcol[:, j:j + 1]], [pv_ps[j]])
                        v.reciprocal(rcol[:, j:j + 1], pv_ps[j][:, 128:129])
                        dst = attn_sb[:, (4 * qb + j) * 256 + h * 128:
                                      (4 * qb + j) * 256 + h * 128 + 128]
                        TV.pre(v, [dst], [pv_ps[j], rcol[:, j:j + 1]])
                        v.tensor_scalar(dst, pv_ps[j][:, 0:128],
                                        rcol[:, j:j + 1], None, ALU.mult)
                    v.drain()
                    TV.clear()
                    v.sem_inc(s_pvd, 1)
            v.sem_inc(s_v, 1)
            C["v"] += 1
        V_ATTN = C["v"]

        if want_taps:
            @block.sync
            def _(sync):
                sync.wait_ge(s_v, V_ATTN)
                dsync(sync, taps["attn"][:], attn_sb[:], "lt")
        LT_ATTN = C["lt"]

        if LVL == 1:
            finish([(s_v, V_ATTN)])
            c_es.close()
            qkv_es.close()
            return nc, taps

        c_es.close()
        qkv_es.close()
        x_res = es.enter_context(nc.sbuf_tensor("x_res", [128, NPT * D], F32))
        d_es = ExitStack()
        attnT = sbt(d_es, "attnT", [128, NDC * TC], BF16)
        woT_bf = sbt(d_es, "woT_bf", [128, NDC * D], BF16)
        pwo = [pst(d_es, f"pwo{i}", [128, 512]) for i in range(2)]

        @block.gpsimd
        def _(g):
            g.wait_ge(s_v, V_ATTN)
            dsync(g, cc2_in.rearrange("(c p) o -> p c o", p=128),
                  attn_sb[:].rearrange("p (c o) -> p c o", c=64), "ldg")
            g.wait_ge(s_ldg, C["ldg"])
            if want_taps:
                g.wait_ge(s_lt, LT_ATTN)
            for q in range(4):
                dsync(g, woT_bf[:, q * 4 * D:(q + 1) * 4 * D]
                      .rearrange("p (hc o) -> p hc o", hc=4),
                      woT_in[q * 512:(q + 1) * 512, :]
                      .rearrange("(hc p) o -> p hc o", p=128), "ldg")
            g.collective_compute(
                "AllGather", ALU.bypass, replica_groups=[list(range(NCORE))],
                ins=[cc2_in[:]], outs=[cc2_out[:]]).then_inc(s_cc, 1)
            C["cc"] += 1
        CC2 = C["cc"]
        LDG_WO = C["ldg"]

        @block.sync
        def _(sync):
            sync.wait_ge(s_v, V_ATTN)
            dsync(sync, x_res[:].rearrange("p (t d) -> p t d", t=NPT),
                  x_in.rearrange("(t p) d -> p t d", p=128), "lxr")
        LD_XRES = C["lxr"]

        @block.scalar
        def _(s):
            s.wait_ge(s_cc, CC2)
            s.wait_ge(s_lc, LD_CONST)
            TS.clear()
            reg = s.value_load(pid_sb[0:1, 0:1])
            for arm in s.Switch(reg, NCORE):
                for hg in range(16):
                    r, hh = divmod(hg, 2)
                    s.dma_start_transpose(
                        attnT[:, hg * TC:(hg + 1) * TC],
                        cc2_out[r * T + arm * TC:r * T + (arm + 1) * TC,
                                hh * 128:(hh + 1) * 128]).then_inc(s_lds, 16)
            C["lds"] += 16 * 16
        LDS_AT = C["lds"]

        @block.tensor
        def _(t):
            t.wait_ge(s_lds, LDS_AT)
            t.wait_ge(s_ldg, LDG_WO)
            for p in range(NPT):
                for oc in range(4):
                    u = p * 4 + oc
                    if u >= 2:
                        t.wait_ge(s_wov, u - 1)
                    for hg in range(16):
                        mm = t.matmul(
                            pwo[u % 2][:],
                            attnT[:, hg * TC + p * 128:hg * TC + p * 128 + 128],
                            woT_bf[:, hg * D + oc * 512:hg * D + oc * 512 + 512],
                            start=(hg == 0), stop=(hg == 15))
                    mm.then_inc(s_wo, 1)

        @block.vector
        def _(v):
            v.wait_ge(s_lxr, LD_XRES)
            for p in range(NPT):
                for oc in range(4):
                    u = p * 4 + oc
                    v.wait_ge(s_wo, u + 1)
                    TV.clear()
                    xsl = x_res[:, p * D + oc * 512:p * D + oc * 512 + 512]
                    TV.pre(v, [xsl], [xsl, pwo[u % 2]])
                    v.tensor_tensor(xsl, xsl, pwo[u % 2][:], ALU.add
                                    ).then_inc(s_wov, 1)
            v.drain()
            TV.clear()
            v.sem_inc(s_v, 1)
            C["v"] += 1
        V_WO = C["v"]

        if want_taps:
            @block.sync
            def _(sync):
                sync.wait_ge(s_v, V_WO)
                dsync(sync, taps["wo"][:], x_res[:], "lt")
        LT_WO = C["lt"]

        if LVL == 2:
            finish([(s_v, V_WO)])
            d_es.close()
            return nc, taps

        d_es.close()

        e1 = ExitStack()
        xln_bf = sbt(e1, "xln_bf", [128, NPT * D], BF16)
        escr = sbt(e1, "escr", [128, D])
        elncol = sbt(e1, "elncol", [128, 40])
        e2 = ExitStack()
        xlnT = sbr(e2, "xlnT", [128, NDC * TC], BF16)
        epw_es = ExitStack()
        epw = [pst(epw_es, f"epw{i}", [128, 128], BF16) for i in range(2)]

        @block.vector
        def _(v):
            v.wait_ge(s_v, V_WO)
            if want_taps:
                v.wait_ge(s_lt, LT_WO)
            TV.clear()
            for p in range(NPT):
                x_t = x_res[:, p * D:(p + 1) * D]
                TV.pre(v, [escr[:], elncol[:, 8 + p:9 + p]], [x_t])
                v.scalar_tensor_tensor(escr[:], x_t, 1.0, x_t, ALU.bypass,
                                       ALU.mult, accum_out=elncol[:, 8 + p:9 + p])
                TV.pre(v, [elncol[:, p:p + 1]], [x_t])
                v.reduce_sum(elncol[:, p:p + 1].rearrange("p (a o) -> p a o", a=1),
                             x_t.rearrange("p (a d) -> p a d", a=1), axis=AX)
            TV.pre(v, [elncol[:, 0:8]], [elncol[:, 0:8]])
            v.tensor_scalar(elncol[:, 0:8], elncol[:, 0:8], 1.0 / D, None, ALU.mult)
            TV.pre(v, [elncol[:, 32:40]], [elncol[:, 0:8]])
            v.tensor_tensor(elncol[:, 32:40], elncol[:, 0:8], elncol[:, 0:8],
                            ALU.mult)
            TV.pre(v, [elncol[:, 8:16]], [elncol[:, 8:16], elncol[:, 32:40]])
            v.scalar_tensor_tensor(elncol[:, 8:16], elncol[:, 8:16], 1.0 / D,
                                   elncol[:, 32:40], ALU.mult, ALU.subtract)
            TV.pre(v, [elncol[:, 8:16]], [elncol[:, 8:16]])
            v.tensor_scalar(elncol[:, 8:16], elncol[:, 8:16], 1e-5, None, ALU.add)
            v.drain()
            TV.clear()
            v.sem_inc(s_v, 1)
            C["v"] += 1
        V_LN2S = C["v"]

        @block.scalar
        def _(s):
            s.wait_ge(s_v, V_LN2S)
            TS.clear()
            TS.pre(s, [elncol[:, 16:24]], [elncol[:, 8:16], cp])
            s.activation(elncol[:, 16:24], elncol[:, 8:16], ACT.Sqrt,
                         bias=cp[:, 3:4], scale=cp[:, 1:2])
            s.drain()
            TS.clear()
            s.sem_inc(s_s, 1)
            C["s"] += 1
        S_LN2 = C["s"]

        @block.vector
        def _(v):
            v.wait_ge(s_s, S_LN2)
            TV.clear()
            TV.pre(v, [elncol[:, 24:32]], [elncol[:, 16:24]])
            v.reciprocal(elncol[:, 24:32], elncol[:, 16:24])
            for p in range(NPT):
                x_t = x_res[:, p * D:(p + 1) * D]
                dst = xln_bf[:, p * D:(p + 1) * D]
                TV.pre(v, [dst],
                       [x_t, elncol[:, p:p + 1], elncol[:, 24 + p:25 + p]])
                v.tensor_scalar(dst, x_t, elncol[:, p:p + 1],
                                elncol[:, 24 + p:25 + p], ALU.subtract, ALU.mult)
            v.drain()
            TV.clear()
            v.sem_inc(s_v, 1)
            C["v"] += 1
        V_XLN = C["v"]

        etr = []
        for p in range(NPT):
            for dc in range(NDC):
                etr.append((xln_bf[:, p * D + dc * 128:p * D + dc * 128 + 128],
                            xlnT[:, dc * TC + p * 128:dc * TC + p * 128 + 128]))
        emit_transposes(etr, V_XLN, epw)

        @block.vector
        def _(v):
            v.wait_ge(s_te, NTR[0])
            TV.clear()
            v.sem_inc(s_v, 1)
            C["v"] += 1
        V_XLNT = C["v"]
        e1.close()
        e3 = ExitStack()
        h_sb = sbt(e3, "h_sb", [128, NIC * 512], BF16)

        sh_bf = sbt(e3, "sh_bf", [1, 1024], BF16)
        gu_es = ExitStack()
        g_ps = [pst(gu_es, f"g_ps{i}", [128, 512]) for i in range(2)]
        u_ps = [pst(gu_es, f"u_ps{i}", [128, 512]) for i in range(2)]
        s1_ps = [pst(gu_es, f"s1_ps{i}", [1, 512]) for i in range(2)]
        s1row = sbt(gu_es, "s1row", [1, 1024])
        s1_bf = sbt(gu_es, "s1_bf", [1, 1024], BF16)
        gwb = sbt(gu_es, "gwb", [128, 2 * D], BF16)
        uwb = sbt(gu_es, "uwb", [128, 2 * D], BF16)
        sg_sb = [sbt(gu_es, f"sg{i}", [128, 512], BF16) for i in range(2)]
        h1st = [sbt(gu_es, f"h1st{i}", [128, 512], BF16) for i in range(2)]

        @block.tensor
        def _(t):
            t.wait_ge(s_v, V_XLNT)
            for b2 in range(2):
                for dc in range(NDC):
                    mm = t.matmul(s1_ps[b2][:], ones_bf[:],
                                  xlnT[:, dc * TC + b2 * 512:dc * TC + b2 * 512 + 512],
                                  start=(dc == 0), stop=(dc == NDC - 1))
                mm.then_inc(s_f4, 1)

        @block.vector
        def _(v):
            v.wait_ge(s_f4, 2)
            TV.clear()
            for b2 in range(2):
                TV.pre(v, [s1row[:]], [s1_ps[b2]])
                v.tensor_copy(s1row[0:1, b2 * 512:(b2 + 1) * 512], s1_ps[b2][0:1, :])
            TV.pre(v, [s1_bf[:]], [s1row[:]])
            v.tensor_copy(s1_bf[0:1, :], s1row[0:1, :])
            v.drain()
            TV.clear()
            v.sem_inc(s_v, 1)
            C["v"] += 1
        V_S1 = C["v"]

        @block.gpsimd
        def _(g):
            for i in range(NIC):
                gsem = s_ga if i % 2 == 0 else s_gb
                g.wait_ge(s_v, V_XLNT)
                if i >= 2:
                    g.wait_ge(s_f1, 2 * i - 2)
                g.dma_start(out=gwb[:, (i % 2) * D:(i % 2 + 1) * D]
                            .rearrange("p (dc w) -> p dc w", dc=NDC),
                            in_=gwT_in[i].rearrange("(dc p) w -> p dc w", p=128)
                            ).then_inc(gsem, 16)
                g.dma_start(out=uwb[:, (i % 2) * D:(i % 2 + 1) * D]
                            .rearrange("p (dc w) -> p dc w", dc=NDC),
                            in_=uwT_in[i].rearrange("(dc p) w -> p dc w", p=128)
                            ).then_inc(gsem, 16)
        LDG_GU = lambda i: 32 * (i // 2 + 1)

        @block.tensor
        def _(t):
            t.wait_ge(s_v, V_S1)
            for i in range(NIC):
                t.wait_ge(s_ga if i % 2 == 0 else s_gb, LDG_GU(i))
                for b2 in range(2):
                    u = 2 * i + b2
                    if u >= 2:
                        t.wait_ge(s_f3, u - 1)
                    for dc in range(NDC):
                        t.matmul(g_ps[b2][:],
                                 gwb[:, (i % 2) * D + dc * 128:
                                     (i % 2) * D + dc * 128 + 128],
                                 xlnT[:, dc * TC + b2 * 512:dc * TC + b2 * 512 + 512],
                                 start=(dc == 0), stop=False)
                    t.matmul(g_ps[b2][:], m128_bf[0:1, :],
                             s1_bf[0:1, b2 * 512:(b2 + 1) * 512],
                             start=False, stop=True)
                    for dc in range(NDC):
                        t.matmul(u_ps[b2][:],
                                 uwb[:, (i % 2) * D + dc * 128:
                                     (i % 2) * D + dc * 128 + 128],
                                 xlnT[:, dc * TC + b2 * 512:dc * TC + b2 * 512 + 512],
                                 start=(dc == 0), stop=False)
                    mm = t.matmul(u_ps[b2][:], m128_bf[0:1, :],
                                  s1_bf[0:1, b2 * 512:(b2 + 1) * 512],
                                  start=False, stop=True)
                    mm.then_inc(s_f1, 1)
                    if u >= 2:
                        uo = u - 2
                        io, b2o = divmod(uo, 2)
                        t.wait_ge(s_f3, uo + 1)
                        src = (h_sb[:, io * 512:(io + 1) * 512] if b2o == 0
                               else h1st[io % 2][:])
                        mm = t.matmul(s1_ps[b2o][:], ones_bf[:], src,
                                      start=(io == 0), stop=(io == NIC - 1))
                        mm.then_inc(s_f4, 1)
            for uo in (2 * NIC - 2, 2 * NIC - 1):
                io, b2o = divmod(uo, 2)
                t.wait_ge(s_f3, uo + 1)
                src = (h_sb[:, io * 512:(io + 1) * 512] if b2o == 0
                       else h1st[io % 2][:])
                mm = t.matmul(s1_ps[b2o][:], ones_bf[:], src,
                              start=(io == 0), stop=(io == NIC - 1))
                mm.then_inc(s_f4, 1)

        @block.scalar
        def _(s):
            for i in range(NIC):
                for b2 in range(2):
                    u = 2 * i + b2
                    s.wait_ge(s_f1, u + 1)
                    if u >= 2:
                        s.wait_ge(s_f3, u - 1)
                    TS.clear()
                    TS.pre(s, [sg_sb[u % 2][:]], [g_ps[b2], gs_sb])
                    s.activation(sg_sb[u % 2][:], g_ps[b2][:], ACT.Silu,
                                 bias=cp[:, 3:4], scale=gs_sb[:, i:i + 1]
                                 ).then_inc(s_f2, 1)

        @block.vector
        def _(v):
            for i in range(NIC):
                for b2 in range(2):
                    u = 2 * i + b2
                    v.wait_ge(s_f2, u + 1)
                    TV.clear()
                    if b2 == 0:
                        dst = h_sb[:, i * 512:(i + 1) * 512]
                    else:
                        if i >= 2:
                            v.wait_ge(s_ha if i % 2 == 0 else s_hb,
                                      16 * (i // 2))
                            v.wait_ge(s_f4, 2 + 2 * i - 2)
                        dst = h1st[i % 2][:]
                    TV.pre(v, [dst], [u_ps[b2], us_sb, sg_sb[u % 2]])
                    v.scalar_tensor_tensor(dst, u_ps[b2][:], us_sb[:, i:i + 1],
                                           sg_sb[u % 2][:], ALU.mult, ALU.mult
                                           ).then_inc(s_f3, 1)

        @block.sync
        def _(sync):
            for i in range(NIC):
                sync.wait_ge(s_f3, 2 * i + 2)
                sync.dma_start(out=hscr[i], in_=h1st[i % 2][:]).then_inc(
                    s_ha if i % 2 == 0 else s_hb, 16)
        FH_ST = 16 * (NIC // 2)

        @block.vector
        def _(v):
            v.wait_ge(s_f4, 2 + 2 * NIC)
            TV.clear()
            for b2 in range(2):
                TV.pre(v, [s1row[:]], [s1_ps[b2]])
                v.tensor_copy(s1row[0:1, b2 * 512:(b2 + 1) * 512], s1_ps[b2][0:1, :])
            TV.pre(v, [sh_bf[:]], [s1row[:]])
            v.tensor_copy(sh_bf[0:1, :], s1row[0:1, :])
            v.drain()
            TV.clear()
            v.sem_inc(s_v, 1)
            C["v"] += 1
        V_SH = C["v"]

        gu_es.close()
        e2.close()

        dn_es = ExitStack()
        dn_ps = [pst(dn_es, f"dn{j}", [128, 512]) for j in range(4)]
        wdb = [sbt(dn_es, f"wdb{i}", [128, 16 * 512], BF16) for i in range(2)]
        h1ld = [sbt(dn_es, f"h1ld{i}", [128, 8 * 512], BF16) for i in range(2)]
        stg_bf = sbt(dn_es, "stg_bf", [128, 4 * 512], BF16)

        def dn_cnt(u):
            return 65 * (u // 64) + (u % 64) + 1

        LDG_DN0 = LDG_GU(NIC - 1)
        tr_base = NTR[0]

        NB_D = 8 * NIC // 16

        @block.gpsimd
        def _(g):
            g.wait_ge(s_ha, FH_ST)
            g.wait_ge(s_hb, FH_ST)
            for gb in range(NB_D):
                gi0 = gb * 16
                bq, i0 = divmod(gi0, NIC)
                b, qr = divmod(bq, 4)
                dsem = s_da if gb % 2 == 0 else s_db
                g.wait_ge(s_v, V_SH)
                if gb >= 2:
                    gprev = gb * 16 - 32 + 15
                    g.wait_ge(s_f6, dn_cnt(gprev))
                g.dma_start(out=wdb[gb % 2][:]
                            .rearrange("p (i w) -> p i w", i=16),
                            in_=dwT_in[i0:i0 + 16, :, qr * 512:(qr + 1) * 512]
                            .rearrange("i p w -> p i w")).then_inc(dsem, 16)
        LDG_DN = lambda gb: 16 * (gb // 2 + 1)

        @block.sync
        def _(sync):
            sync.wait_ge(s_ha, FH_ST)
            sync.wait_ge(s_hb, FH_ST)
            for lb in range(4 * NIC // 8):
                qr, i0 = divmod(lb * 8, NIC)
                fsem = s_fla if lb % 2 == 0 else s_flb
                sync.wait_ge(s_v, V_SH)
                if lb >= 2:
                    ug = (4 + qr) * NIC + i0 - 16 + 7
                    sync.wait_ge(s_f6, dn_cnt(ug))
                sync.dma_start(out=h1ld[lb % 2][:]
                               .rearrange("p (i w) -> p i w", i=8),
                               in_=hscr[i0:i0 + 8].rearrange("i p w -> p i w")
                               ).then_inc(fsem, 16)

        @block.tensor
        def _(t):
            t.wait_ge(s_v, V_SH)
            for bq in range(8):
                b, qr = divmod(bq, 4)
                for i in range(NIC):
                    gi = bq * NIC + i
                    if i % 16 == 0:
                        gb = gi // 16
                        t.wait_ge(s_da if gb % 2 == 0 else s_db, LDG_DN(gb))
                    if b == 1 and i % 8 == 0:
                        lb = (qr * NIC + i) // 8
                        t.wait_ge(s_fla if lb % 2 == 0 else s_flb,
                                  16 * (lb // 2 + 1))
                    if i == 0:
                        t.wait_ge(s_f5, bq * 4)
                    h_src = (h_sb[:, i * 512:(i + 1) * 512] if b == 0
                             else h1ld[(i // 8) % 2][:, (i % 8) * 512:
                                                      (i % 8) * 512 + 512])
                    wsl = wdb[(i // 16) % 2][:, (i % 16) * 512:(i % 16) * 512 + 512]
                    for ocl in range(4):
                        mm = t.matmul(dn_ps[ocl][:],
                                      wsl[:, ocl * 128:(ocl + 1) * 128],
                                      h_src, start=(i == 0), stop=False)
                    mm.then_inc(s_f6, 1)
                for ocl in range(4):
                    mm = t.matmul(dn_ps[ocl][:], m128_bf[0:1, :],
                                  sh_bf[0:1, b * 512:(b + 1) * 512],
                                  start=False, stop=True)
                mm.then_inc(s_f6, 1)
                t.wait_ge(s_f5, bq * 4 + 4)
                for k in range(16):
                    m = tr_base + bq * 16 + k
                    if m >= 2:
                        t.wait_ge(s_te, m - 1)
                    ocl, j4 = divmod(k, 4)
                    t.transpose(epw[m % 2][:],
                                stg_bf[:, ocl * 512 + j4 * 128:
                                       ocl * 512 + j4 * 128 + 128],
                                ident[:]).then_inc(s_tp, 1)

        @block.vector
        def _(v):
            v.wait_ge(s_ha, FH_ST)
            v.wait_ge(s_hb, FH_ST)
            for bq in range(8):
                b, qr = divmod(bq, 4)
                v.wait_ge(s_f6, 65 * bq + 65)
                if bq >= 1:
                    v.wait_ge(s_tp, tr_base + 16 * bq)
                TV.clear()
                for ocl in range(4):
                    oc_g = qr * 4 + ocl
                    dst = stg_bf[:, ocl * 512:(ocl + 1) * 512]
                    TV.pre(v, [dst], [dn_ps[ocl], ds_sb])
                    v.tensor_scalar(dst, dn_ps[ocl][:], ds_sb[:, oc_g:oc_g + 1],
                                    None, ALU.mult).then_inc(s_f5, 1)
                for k in range(16):
                    m = tr_base + bq * 16 + k
                    v.wait_ge(s_tp, m + 1)
                    TV.clear()
                    ocl, j4 = divmod(k, 4)
                    oc_g = qr * 4 + ocl
                    p_own = 4 * b + j4
                    xsl = x_res[:, p_own * D + oc_g * 128:
                                p_own * D + oc_g * 128 + 128]
                    TV.pre(v, [xsl], [xsl, epw[m % 2]])
                    v.tensor_tensor(xsl, xsl, epw[m % 2][:], ALU.add
                                    ).then_inc(s_te, 1)
            v.drain()
            TV.clear()
            v.sem_inc(s_v, 1)
            C["v"] += 1
        V_FFN = C["v"]
        NTR[0] = tr_base + 8 * 16

        @block.sync
        def _(sync):
            sync.wait_ge(s_v, V_FFN)
            dsync(sync, y_out.rearrange("(t p) d -> p t d", p=128),
                  x_res[:].rearrange("p (t d) -> p t d", t=NPT), "out")

        finish([(s_v, V_FFN)])
        dn_es.close()
        epw_es.close()
        e3.close()

    return nc, taps


_CACHE = {}


def _prog(upto="full", want_taps=False):
    key = (upto, want_taps)
    if key not in _CACHE:
        _CACHE[key] = build_program(upto, want_taps)
    return _CACHE[key]


def make_inputs(L0_attn_x, L0_attn_wq, L0_attn_wk, L0_attn_wv, L0_attn_wo,
                L0_ffn_gate_w, L0_ffn_gate_scale, L0_ffn_up_w, L0_ffn_up_scale,
                L0_ffn_down_w, L0_ffn_down_scale):
    bf = ml_dtypes.bfloat16
    x = np.ascontiguousarray(np.asarray(L0_attn_x, dtype=np.float32)[0])
    wq = np.asarray(L0_attn_wq, dtype=np.float32)
    wk = np.asarray(L0_attn_wk, dtype=np.float32)
    wv = np.asarray(L0_attn_wv, dtype=np.float32)
    woT = np.ascontiguousarray(np.asarray(L0_attn_wo, dtype=np.float32).T)
    gw = np.asarray(L0_ffn_gate_w, dtype=np.int32)
    uw = np.asarray(L0_ffn_up_w, dtype=np.int32)
    dw = np.asarray(L0_ffn_down_w, dtype=np.int32)
    gwT = np.ascontiguousarray(gw.T.reshape(D, NIC, 128).transpose(1, 0, 2))
    uwT = np.ascontiguousarray(uw.T.reshape(D, NIC, 128).transpose(1, 0, 2))
    dwT = np.ascontiguousarray(dw.T.reshape(NIC, 128, D))
    gs = np.ascontiguousarray(
        np.asarray(L0_ffn_gate_scale, np.float32).reshape(NIC, 128).T)
    us = np.ascontiguousarray(
        np.asarray(L0_ffn_up_scale, np.float32).reshape(NIC, 128).T)
    ds = np.ascontiguousarray(
        np.asarray(L0_ffn_down_scale, np.float32).reshape(NDC, 128).T)
    a_ = np.arange(0, HD, 2, dtype=np.float32)
    inv = (np.float32(1.0) / (np.float32(1e6) ** (a_ / np.float32(HD)))
           ).astype(np.float32)
    pos = np.arange(T, dtype=np.float32)
    fr = pos[:, None] * inv[None, :]
    emb = np.concatenate([fr, fr], axis=1).astype(np.float32)
    cos_t = np.cos(emb).astype(np.float32)
    sin_t = np.sin(emb).astype(np.float32)
    ident = np.eye(128, dtype=np.float32).astype(bf)
    trim = (np.arange(128)[:, None] <= np.arange(128)[None, :]).astype(bf)

    in_maps = []
    for c in range(NCORE):
        kv = c // 2
        in_maps.append({
            "x_in": np.ascontiguousarray(x[TC * c:TC * (c + 1)]),
            "wq_in": np.ascontiguousarray(wq[256 * c:256 * (c + 1)]),
            "wk_in": np.ascontiguousarray(wk[128 * kv:128 * (kv + 1)]),
            "wv_in": np.ascontiguousarray(wv[128 * kv:128 * (kv + 1)]),
            "woT_in": woT,
            "gwT_in": gwT, "uwT_in": uwT, "dwT_in": dwT,
            "gs_in": gs, "us_in": us, "ds_in": ds,
            "cos_in": cos_t, "sin_in": sin_t,
            "ident_in": ident, "tri_in": trim,
        })
    return in_maps


def kernel(**inputs):
    in_maps = make_inputs(**inputs)
    nc, _ = _prog("full", False)
    res = run_bass_kernel_spmd(nc, in_maps, core_ids=list(range(NCORE)))
    y = np.concatenate([res.results[c]["y_out"] for c in range(NCORE)], axis=0)
    return y.reshape(1, T, D)
